# version 10
# baseline (speedup 1.0000x reference)
import numpy as np
from contextlib import ExitStack

import concourse.bass as bass
import concourse.mybir as mybir
from concourse.bass_utils import run_bass_kernel_spmd

F32 = mybir.dt.float32
BF16 = mybir.dt.bfloat16
I32 = mybir.dt.int32
AF = mybir.ActivationFunctionType
ALU = mybir.AluOpType
AX = mybir.AxisListType

D = 2048
DFF = 8192
T = 2048
S = 4096
H = 16
NT = 16
NTS = 32
EPS_LN = 1e-5
EPS_Z = 1e-6
QCH = 512
NCORES = 8


class StopEmission(Exception):
    pass


def build_program(taps=(), limit=99, no_quant=False):
    taps = set(taps)
    nc = bass.Bass("TRN2", target_bir_lowering=False)

    x_h = nc.dram_tensor("x", [S, D], F32, kind="ExternalInput").ap()
    wqT_h = nc.dram_tensor("wqT", [D, D], F32, kind="ExternalInput").ap()
    wkT_h = nc.dram_tensor("wkT", [D, D], F32, kind="ExternalInput").ap()
    wvT_h = nc.dram_tensor("wvT", [D, D], F32, kind="ExternalInput").ap()
    woT_h = nc.dram_tensor("woT", [D, D], F32, kind="ExternalInput").ap()
    w1T_h = nc.dram_tensor("w1T", [D, DFF], F32, kind="ExternalInput").ap()
    w2T_h = nc.dram_tensor("w2T", [DFF, D], F32, kind="ExternalInput").ap()
    bq_h = nc.dram_tensor("bq", [1, D], F32, kind="ExternalInput").ap()
    bk_h = nc.dram_tensor("bk", [1, D], F32, kind="ExternalInput").ap()
    bv_h = nc.dram_tensor("bv", [1, D], F32, kind="ExternalInput").ap()
    bo_h = nc.dram_tensor("bo", [1, D], F32, kind="ExternalInput").ap()
    out_h = nc.dram_tensor("out", [T, D], F32, kind="ExternalOutput").ap()
    x1_scr = nc.dram_tensor("x1_scr", [T, D], F32).ap()
    w1q_scr = nc.dram_tensor("w1q_scr", [16, 16, 128, 512], BF16).ap()
    w2q_scr = nc.dram_tensor("w2q_scr", [64, 128, D], BF16).ap()
    tap_t = {}
    for tname, shape in [("xn", [128, 16 * T]), ("kv", [128, H * 132]),
                         ("qe", [128, 16 * T]), ("attn", [128, 16 * T]),
                         ("h0", [128, 16 * T]), ("rr", [16, T]),
                         ("n1f", [128, 16 * T]), ("hs0", [128, 16 * T])]:
        if tname in taps:
            tap_t[tname] = nc.dram_tensor("tap_" + tname, shape, BF16,
                                          kind="ExternalOutput").ap()
    if "x1" in taps:
        tap_t["x1"] = nc.dram_tensor("tap_x1", [T, D], F32,
                                     kind="ExternalOutput").ap()
    if "wsc" in taps:
        tap_t["wsc"] = nc.dram_tensor("tap_wsc", [128, 8], F32,
                                      kind="ExternalOutput").ap()
    if "colsA" in taps:
        tap_t["colsA"] = nc.dram_tensor("tap_colsA", [128, 64], F32,
                                        kind="ExternalOutput").ap()
    if "colsB" in taps:
        tap_t["colsB"] = nc.dram_tensor("tap_colsB", [128, 64], F32,
                                        kind="ExternalOutput").ap()
    if "scr1" in taps:
        tap_t["scr1"] = nc.dram_tensor("tap_scr1", [128, D], F32,
                                       kind="ExternalOutput").ap()
    if "xtap" in taps:
        tap_t["xtap"] = nc.dram_tensor("tap_xtap", [128, D], F32,
                                       kind="ExternalOutput").ap()

    es = ExitStack()
    try:
      with es:
        sbuf = lambda *aa: es.enter_context(nc.sbuf_tensor(*aa))

        ident_bf = sbuf("ident_bf", [128, 128], BF16)
        ident_f32 = sbuf("ident_f32", [128, 128], F32)
        ones_bf = sbuf("ones_bf", [128, 512], BF16)
        ones_f32 = sbuf("ones_f32", [128, 128], F32)
        ones_col = sbuf("ones_col", [128, 1], F32)
        biasA = sbuf("biasA", [128, D], BF16)
        biasB = sbuf("biasB", [128, D], BF16)
        scr1 = sbuf("scr1", [128, D], F32)
        scrE1 = sbuf("scrE1", [128, 512], F32)
        scrE2 = sbuf("scrE2", [128, 512], F32)
        scrE3 = sbuf("scrE3", [128, 2, 512], F32)
        scri = sbuf("scri", [128, 1024], I32)
        cols = sbuf("cols", [128, 64], F32)
        qrow = sbuf("qrow", [16, 8], F32)
        wsc = sbuf("wsc", [128, 8], F32)
        abscol = sbuf("abscol", [128, 4], F32)
        xt = sbuf("xt", [128, 2, D], F32)
        B1 = sbuf("B1", [128, 16, T], BF16)
        B2 = sbuf("B2", [128, 16, T], BF16)

        C = {"sems": {}, "cnt": {}, "waited": {}}

        def sem(name):
            if name not in C["sems"]:
                C["sems"][name] = es.enter_context(nc.semaphore(name))
                C["cnt"][name] = 0
            return C["sems"][name]

        def inc(instr, name, k=1):
            instr.then_inc(sem(name), k)
            C["cnt"][name] += k
            return C["cnt"][name]

        def cnt(name):
            return C["cnt"].get(name, 0)

        def wait(eng, ename, name, val=None):
            if val is None:
                val = cnt(name)
            if val is None or val <= 0:
                return
            key = (ename, name)
            if C["waited"].get(key, -1) >= val:
                return
            eng.wait_ge(sem(name), val)
            C["waited"][key] = val

        if True:
            sp, g, pe, v, a = nc.sync, nc.gpsimd, nc.tensor, nc.vector, nc.scalar

            def gS(instr):
                c = inc(instr, "g_f2")
                wait(g, "g", "g_f2", c)
                return c

            def vS(instr):
                c = inc(instr, "v")
                wait(v, "v", "v", c)
                return c

            def aS(instr):
                c = inc(instr, "a")
                wait(a, "a", "a", c)
                return c

            def ckpt(n):
                if limit <= n:
                    inc(sp.dma_start(out=out_h[0:128, :], in_=scr1[:, 0:D]),
                        "d_out", 16)
                    wait(sp, "sp", "d_out")
                    raise StopEmission()

            _CKPT = ckpt
            g.memset(ident_bf[:], 0.0)
            g.affine_select(out=ident_bf[:], in_=ident_bf[:],
                            compare_op=ALU.not_equal, fill=1.0, base=0,
                            pattern=[[-1, 128]], channel_multiplier=1)
            g.memset(ident_f32[:], 0.0)
            g.affine_select(out=ident_f32[:], in_=ident_f32[:],
                            compare_op=ALU.not_equal, fill=1.0, base=0,
                            pattern=[[-1, 128]], channel_multiplier=1)
            ckpt(0.005)
            g.memset(ones_bf[:], 1.0)
            g.memset(ones_f32[:], 1.0)
            g.memset(ones_col[:], 1.0)
            ckpt(0.01)
            g.memset(biasA[:], 0.0)
            inc(g.memset(biasB[:], 0.0), "g")
            ckpt(0.02)
            inc(g.dma_start(out=biasA[0:1, :], in_=bq_h), "dc", 16)
            ckpt(0.03)
            inc(g.dma_start(out=biasA[32:33, :], in_=bk_h), "dc", 16)
            inc(g.dma_start(out=biasA[64:65, :], in_=bv_h), "dc", 16)
            inc(g.dma_start(out=biasB[0:1, :], in_=bo_h), "dc", 16)
            ckpt(0.05)

            def ln_stats(src_ap, want_maxmin=False, width=D):
                v.tensor_reduce(cols[:, 0:1], src_ap, AX.X, ALU.add)
                if want_maxmin:
                    v.tensor_reduce(cols[:, 6:7], src_ap, AX.X, ALU.max)
                    v.tensor_reduce(cols[:, 7:8], src_ap, AX.X, ALU.min)
                wait(a, "a", "v")
                inc(a.activation(out=scr1[:, 0:width], in_=src_ap,
                                 func=AF.Square), "a")
                wait(v, "v", "a")
                vS(v.tensor_reduce(cols[:, 1:2], scr1[:, 0:width], AX.X, ALU.add))
                ninv = 1.0 / float(width)
                vS(v.tensor_scalar_mul(cols[:, 2:3], cols[:, 0:1], ninv))
                vS(v.tensor_scalar_mul(cols[:, 3:4], cols[:, 1:2], ninv))
                vS(v.tensor_tensor(cols[:, 8:9], cols[:, 2:3], cols[:, 2:3],
                                   ALU.mult))
                vS(v.tensor_tensor(cols[:, 3:4], cols[:, 3:4], cols[:, 8:9],
                                   ALU.subtract))
                inc(v.tensor_scalar_add(cols[:, 3:4], cols[:, 3:4], EPS_LN), "v")
                wait(a, "a", "v")
                aS(a.activation(out=cols[:, 3:4], in_=cols[:, 3:4], func=AF.Ln))
                inc(a.activation(out=cols[:, 4:5], in_=cols[:, 3:4],
                                 func=AF.Exp, scale=-0.5), "a")
                wait(v, "v", "a")
                vS(v.tensor_scalar_mul(cols[:, 5:6], cols[:, 2:3], -1.0))
                inc(v.tensor_tensor(cols[:, 5:6], cols[:, 5:6], cols[:, 4:5],
                                    ALU.mult), "v")
                wait(v, "v", "v")

            def elu_evac(ps_ap, dst_ap, par=[0]):
                p = par[0] = 1 - par[0]
                e3 = scrE3[:, p, :]
                c_ps = inc(v.tensor_copy(e3, ps_ap), "v")
                v.tensor_scalar_min(scrE1[:], e3, 0.0)
                inc(v.tensor_scalar_max(scrE2[:], e3, 0.0), "v")
                wait(a, "a", "v")
                inc(a.activation(out=scrE1[:], in_=scrE1[:], func=AF.Exp), "a")
                wait(v, "v", "a")
                c_add = inc(v.tensor_add(dst_ap, scrE1[:], scrE2[:]), "v")
                return c_ps, c_add

            def gen_abssum(wq_in):
                sub = 0
                freed = [None, None]
                for mi, (wh, nch, fw) in enumerate([(w1T_h, 16, DFF),
                                                    (w2T_h, 64, D)]):
                    first = True
                    for ch in range(nch):
                        for sc in range(fw // QCH):
                            b = sub % 2
                            if freed[b] is not None:
                                wait(sp, "sp", "g_abs", freed[b])
                            inc(sp.dma_start(
                                out=wq_in[:, b, :],
                                in_=wh[128 * ch:128 * ch + 128,
                                       QCH * sc:QCH * sc + QCH]), "d_wabs", 16)
                            wait(v, "v", "d_wabs")
                            c0 = inc(v.tensor_reduce(
                                abscol[:, b:b + 1], wq_in[:, b, :], AX.X,
                                ALU.add, apply_absolute_value=True), "g_abs")
                            freed[b] = c0
                            wait(v, "v", "g_abs", c0)
                            if first:
                                inc(v.tensor_copy(cols[:, 32 + mi:33 + mi],
                                                  abscol[:, b:b + 1]), "g_abs")
                                first = False
                            else:
                                inc(v.tensor_tensor(
                                    cols[:, 32 + mi:33 + mi],
                                    cols[:, 32 + mi:33 + mi],
                                    abscol[:, b:b + 1], ALU.add), "g_abs")
                            wait(v, "v", "g_abs")
                            sub += 1
                            yield

            def quant_finalize(psSm):
                wait(pe, "pe", "g_abs")
                inc(pe.matmul(psSm[0:1, 0:2], ones_col[:], cols[:, 32:34],
                              start=True, stop=True, skip_group_check=True), "pe")
                wait(v, "v", "pe")
                vS(v.tensor_scalar(qrow[0:1, 0:2], psSm[0:1, 0:2],
                                   1.0 / float(DFF * D), 1e-5, ALU.mult, ALU.max))
                vS(v.reciprocal(qrow[0:1, 2:4], qrow[0:1, 0:2]))
                inc(v.tensor_scalar_mul(qrow[0:1, 4:6], qrow[0:1, 0:2],
                                        1.0 / 127.0), "v")
                wait(pe, "pe", "v")
                pe.matmul(psSm[:, 0:1], ones_f32[0:1, :], qrow[0:1, 2:3],
                          start=True, stop=True, skip_group_check=True)
                pe.matmul(psSm[:, 1:2], ones_f32[0:1, :], qrow[0:1, 4:5],
                          start=False, stop=True, skip_group_check=True)
                pe.matmul(psSm[:, 2:3], ones_f32[0:1, :], qrow[0:1, 3:4],
                          start=False, stop=True, skip_group_check=True)
                inc(pe.matmul(psSm[:, 3:4], ones_f32[0:1, :], qrow[0:1, 5:6],
                              start=False, stop=True, skip_group_check=True), "pe")
                wait(v, "v", "pe")
                inc(v.tensor_copy(wsc[:, 0:4], psSm[:, 0:4]), "v_wsc")
                if "wsc" in taps:
                    wait(sp, "sp", "v_wsc")
                    inc(sp.dma_start(out=tap_t["wsc"], in_=wsc[:]), "d_out", 16)

            def gen_round(wq_in, wq_bfb, qscr):
                sub = 0
                freed = [None, None]
                out_d = [None, None]
                for (wh, scr_t, nch, fw, sci) in [(w1T_h, w1q_scr, 16, DFF, 0),
                                                  (w2T_h, w2q_scr, 64, D, 2)]:
                    for ch in range(nch):
                        for sc in range(fw // QCH):
                            b = sub % 2
                            if freed[b] is not None:
                                wait(sp, "sp", "g_q", freed[b])
                            inc(sp.dma_start(
                                out=wq_in[:, b, :],
                                in_=wh[128 * ch:128 * ch + 128,
                                       QCH * sc:QCH * sc + QCH]), "d_wrnd", 16)
                            wait(g, "g", "d_wrnd")
                            wait(g, "g", "v_wsc")
                            gS(g.tensor_scalar(qscr[:], wq_in[:, b, :],
                                               wsc[:, sci:sci + 1], 1.0,
                                               ALU.mult, ALU.min))
                            gS(g.tensor_scalar_max(scri[:, 0:QCH], qscr[:], -1.0))
                            if out_d[b] is not None:
                                wait(g, "g", "d_wqo", out_d[b])
                            freed[b] = inc(g.tensor_copy(wq_bfb[:, b, :],
                                                         scri[:, 0:QCH]), "g_q")
                            wait(sp, "sp", "g_q")
                            if scr_t is w1q_scr:
                                dst = scr_t[sc, ch, :, :]
                            else:
                                dst = scr_t[ch, :, QCH * sc:QCH * sc + QCH]
                            out_d[b] = inc(sp.dma_start(
                                out=dst, in_=wq_bfb[:, b, :]), "d_wqo", 16)
                            sub += 1
                            yield

            def drive(gg, n):
                for _ in range(n):
                    try:
                        next(gg)
                    except StopIteration:
                        return

            def drain(gg):
                for _ in gg:
                    pass

            esK = ExitStack()
            kv_bf = esK.enter_context(nc.sbuf_tensor("kv_bf", [128, H, 132], BF16))
            kblk = esK.enter_context(nc.sbuf_tensor("kblk", [128, H, 16], BF16))
            sel16 = esK.enter_context(nc.sbuf_tensor("sel16", [16, 16, 128], BF16))
            rrow = esK.enter_context(nc.sbuf_tensor("rrow", [16, T], BF16))
            wq_in = esK.enter_context(nc.sbuf_tensor("wq_in", [128, 2, QCH], F32))
            wq_bfb = esK.enter_context(nc.sbuf_tensor("wq_bfb", [128, 2, QCH], BF16))
            qscr = esK.enter_context(nc.sbuf_tensor("qscr", [128, QCH], F32))

            g.memset(kblk[:], 0.0)
            g.memset(sel16[:], 0.0)
            inc(g.affine_select(out=sel16[:], in_=sel16[:],
                                compare_op=ALU.not_equal, fill=1.0, base=0,
                                pattern=[[-1, 16], [0, 128]],
                                channel_multiplier=1), "g")
            ckpt(0.1)

            esA = ExitStack()
            kp_bf = esA.enter_context(nc.sbuf_tensor("kp_bf", [128, 1024], BF16))
            vext = esA.enter_context(nc.sbuf_tensor("vext", [128, 8, 132], BF16))
            xn_tm = esA.enter_context(nc.sbuf_tensor("xn_tm", [128, D], BF16))
            xn_oth = esA.enter_context(nc.sbuf_tensor("xn_oth", [128, 16, 128], BF16))
            psProj = esA.enter_context(nc.psum_tensor("psProj", [128, 2, 512], F32))
            psKV = [esA.enter_context(nc.psum_tensor(f"psKV{i}", [128, 2, 132], F32))
                    for i in range(4)]
            psT_a = esA.enter_context(nc.psum_tensor("psT_a", [128, 4, 128], BF16))
            psT_b = esA.enter_context(nc.psum_tensor("psT_b", [128, 4, 128], BF16))
            psT = [psT_a, psT_b]
            psSm = psProj[:, 0, 0:8]

            inc(g.memset(vext[:], 1.0), "g")

            def load_wkv_pair(p):
                for dc in range(16):
                    inc(g.dma_start(
                        out=B2[:, dc, 0:1024],
                        in_=wkT_h[128 * dc:128 * dc + 128,
                                  1024 * p:1024 * p + 1024]), "dc", 16)
                for dc in range(16):
                    inc(g.dma_start(
                        out=B2[:, dc, 1024:2048],
                        in_=wvT_h[128 * dc:128 * dc + 128,
                                  1024 * p:1024 * p + 1024]), "dc", 16)
                return cnt("dc")

            wkv_ready = load_wkv_pair(0)
            gen_a = gen_abssum(wq_in)
            gen_r = None

            x_freed = [None, None]
            tr_done = [None, None]
            pk_free = None
            pv_free = None
            xn_tm_free = [None]

            def load_x_tile(row, b, src=x_h, tag="d_x"):
                if x_freed[b] is not None:
                    wait(sp, "sp", x_freed[b][0], x_freed[b][1])
                inc(sp.dma_start(out=xt[:, b, :],
                                 in_=src[128 * row:128 * row + 128, :]), tag, 16)

            def build_xn(i, b, dbg=False):
                if dbg:
                    ckpt(0.15)
                wait(v, "v", "d_x")
                wait(a, "a", "d_x")
                ln_stats(xt[:, b, :])
                if dbg:
                    ckpt(0.18)
                if xn_tm_free[0] is not None:
                    wait(v, "v", "pe_tr", xn_tm_free[0])
                c1 = inc(v.tensor_scalar(xn_tm[:], xt[:, b, :], cols[:, 4:5],
                                         cols[:, 5:6], ALU.mult, ALU.add), "v")
                x_freed[b] = ("v", c1)
                if dbg:
                    ckpt(0.19)
                wait(pe, "pe", "v", c1)
                for q in range(4):
                    pj = q % 2
                    if tr_done[pj] is not None:
                        wait(pe, "pe", "v_tr", tr_done[pj])
                    for m in range(4):
                        j = 4 * q + m
                        inc(pe.transpose(psT[pj][:, m, :],
                                         xn_tm[:, 128 * j:128 * j + 128],
                                         ident_bf[:]), "pe_tr")
                    wait(v, "v", "pe_tr")
                    dst = (B1[:, 4 * q:4 * q + 4, 128 * i:128 * i + 128] if i < NT
                           else xn_oth[:, 4 * q:4 * q + 4, :])
                    tr_done[pj] = inc(v.tensor_copy(dst, psT[pj][:]), "v_tr")
                xn_tm_free[0] = cnt("pe_tr")

            for p in range(2):
                if p == 1:
                    if no_quant:
                        ckpt(0.9)
                    quant_finalize(psSm)
                    wait(pe, "pe", "v_wsc")
                    wait(g, "g", "pe")
                    wkv_ready = load_wkv_pair(1)
                    gen_r = gen_round(wq_in, wq_bfb, qscr)
                    wait(pe, "pe", "v_kv")
                for i in range(NTS):
                    b = i % 2
                    own = i < NT
                    if p == 0 or not own:
                        load_x_tile(i, b)
                        build_xn(i, b, dbg=(p == 0 and i == 0))
                    if p == 0 and i == 0:
                        ckpt(0.2)
                    if p == 0 and i == 2:
                        ckpt(0.36)
                    if no_quant:
                        pass
                    elif p == 1:
                        drive(gen_r, 10)
                    xnsrc = ((lambda dc, i=i: B1[:, dc, 128 * i:128 * i + 128])
                             if own else (lambda dc: xn_oth[:, dc, :]))
                    wait(pe, "pe", "dc", wkv_ready)
                    wait(pe, "pe", "v_tr")
                    for s2 in range(2):
                        strip = 2 * p + s2
                        if pk_free is not None:
                            wait(pe, "pe", "v", pk_free[0])
                            if pk_free[1] is not None:
                                wait(pe, "pe", "a", pk_free[1])
                        for dc in range(16):
                            pe.matmul(psProj[:, 0, :], xnsrc(dc),
                                      B2[:, dc, 512 * s2:512 * s2 + 512],
                                      start=(dc == 0), stop=False,
                                      skip_group_check=True)
                        inc(pe.matmul(psProj[:, 0, :], ones_bf[32:33, 0:128],
                                      biasA[32:33, 512 * strip:512 * strip + 512],
                                      start=False, stop=True,
                                      skip_group_check=True), "pe")
                        wait(v, "v", "pe")
                        cps, kp_add = elu_evac(psProj[:, 0, :],
                                               kp_bf[:, 512 * s2:512 * s2 + 512])
                        pk_free = (cps, None)
                        if pv_free is not None:
                            wait(pe, "pe", "a", pv_free)
                        for dc in range(16):
                            pe.matmul(psProj[:, 1, :], xnsrc(dc),
                                      B2[:, dc, 1024 + 512 * s2:1024 + 512 * s2 + 512],
                                      start=(dc == 0), stop=False,
                                      skip_group_check=True)
                        inc(pe.matmul(psProj[:, 1, :], ones_bf[64:65, 0:128],
                                      biasA[64:65, 512 * strip:512 * strip + 512],
                                      start=False, stop=True,
                                      skip_group_check=True), "pe")
                        wait(a, "a", "pe")
                        pv_free = inc(a.activation(
                            out=vext[:, 4 * s2:4 * s2 + 4, 0:128],
                            in_=psProj[:, 1, :].rearrange("p (h f) -> p h f", h=4),
                            func=AF.Copy), "a")
                        wait(pe, "pe", "v", kp_add)
                        wait(pe, "pe", "a", pv_free)
                        for hh in range(4):
                            pst = psKV[2 * s2 + hh // 2]
                            st = (i == 0 and hh % 2 == 0)
                            mm = pe.matmul(
                                pst[:, hh % 2, 0:129],
                                kp_bf[:, 512 * s2 + 128 * hh:
                                      512 * s2 + 128 * hh + 128],
                                vext[:, 4 * s2 + hh, 0:129],
                                start=st, stop=(i == NTS - 1),
                                skip_group_check=True)
                            if hh == 3:
                                inc(mm, "pe")
                            if hh == 3 and p == 0 and not no_quant:
                                drive(gen_a, 8)
                                if i == NTS - 1 and s2 == 1:
                                    drain(gen_a)
                    if p == 0 and i in (0, 1, 2, 3, 15, 17):
                        ckpt({0: 0.3, 1: 0.35, 2: 0.38, 3: 0.4, 15: 0.5, 17: 0.55}[i])
                wait(v, "v", "pe")
                for s2 in range(2):
                    strip = 2 * p + s2
                    for hh in range(4):
                        pst = psKV[2 * s2 + hh // 2]
                        inc(v.tensor_copy(kv_bf[:, 4 * strip + hh, 0:129],
                                          pst[:, hh % 2, 0:129]), "v_kv")
                if p == 0:
                    ckpt(0.7)
            for hh in range(H):
                inc(v.tensor_copy(kblk[:, hh, hh:hh + 1],
                                  kv_bf[:, hh, 128:129]), "v_kv")
            if "xn" in taps:
                wait(sp, "sp", "v_tr")
                inc(sp.dma_start(out=tap_t["xn"],
                                 in_=B1[:].rearrange("p c t -> p (c t)")),
                    "d_out", 16)
            if "kv" in taps:
                wait(sp, "sp", "v_kv")
                inc(sp.dma_start(out=tap_t["kv"],
                                 in_=kv_bf[:].rearrange("p h e -> p (h e)")),
                    "d_out", 16)

            ckpt(1)
            a_end = cnt("pe")
            esA.close()

            esQ = ExitStack()
            wq_sl = esQ.enter_context(nc.sbuf_tensor("wq_sl", [128, 2, 16, 128], BF16))
            psQ = esQ.enter_context(nc.psum_tensor("psQ", [128, 4, 512], F32))

            wait(g, "g", "pe", a_end)

            wq_used = [None, None]

            def load_wq(fc, b):
                if wq_used[b] is not None:
                    wait(g, "g", "pe", wq_used[b])
                inc(g.dma_start(
                    out=wq_sl[:, b, :, :],
                    in_=wqT_h[:, 128 * fc:128 * fc + 128].rearrange(
                        "(c p) f -> p c f", p=128)), "dc", 16)
                return cnt("dc")

            wq_ready = [load_wq(0, 0), None]
            qp_free = [None] * 4
            first_q_evac = True
            wait(pe, "pe", "v_kv")
            wait(pe, "pe", "v")
            wait(pe, "pe", "a")
            for fc in range(16):
                b = fc % 2
                if fc + 1 < 16:
                    wq_ready[(fc + 1) % 2] = load_wq(fc + 1, (fc + 1) % 2)
                wait(pe, "pe", "dc", wq_ready[b])
                for ts4 in range(4):
                    ps = ts4
                    if qp_free[ps] is not None:
                        wait(pe, "pe", "v", qp_free[ps])
                    for dc in range(16):
                        pe.matmul(psQ[:, ps, :], wq_sl[:, b, dc, :],
                                  B1[:, dc, 512 * ts4:512 * ts4 + 512],
                                  start=(dc == 0), stop=False,
                                  skip_group_check=True)
                    inc(pe.matmul(psQ[:, ps, :],
                                  biasA[0:1, 128 * fc:128 * fc + 128],
                                  ones_bf[0:1, 0:512], start=False, stop=True,
                                  skip_group_check=True), "pe")
                    if ts4 == 3:
                        wq_used[b] = cnt("pe")
                    wait(v, "v", "pe")
                    if first_q_evac:
                        wait(v, "v", "pe", a_end)
                        first_q_evac = False
                    qp_free[ps] = elu_evac(psQ[:, ps, :],
                                           B2[:, fc, 512 * ts4:512 * ts4 + 512])[0]
                drive(gen_r, 12)
            drain(gen_r)
            w_all = cnt("d_wqo")
            qe_done = cnt("v")
            q_end = cnt("pe")
            if "qe" in taps:
                wait(sp, "sp", "v", qe_done)
                inc(sp.dma_start(out=tap_t["qe"],
                                 in_=B2[:].rearrange("p c t -> p (c t)")),
                    "d_out", 16)
            ckpt(2)
            esQ.close()

            esZ = ExitStack()
            psZ = esZ.enter_context(nc.psum_tensor("psZ", [16, 512], F32))
            psR = esZ.enter_context(nc.psum_tensor("psR", [128, 2, 512], F32))
            psAt = esZ.enter_context(nc.psum_tensor("psAt", [128, 2, 512], F32))
            rb_sb = esZ.enter_context(nc.sbuf_tensor("rb_sb", [128, 2, 512], F32))

            wait(pe, "pe", "v_kv")
            wait(pe, "pe", "g")
            wait(pe, "pe", "v")
            wait(pe, "pe", "a")
            rr_done = [None] * 4
            for st4 in range(4):
                wait(pe, "pe", "v")
                for c in range(16):
                    mm = pe.matmul(psZ[:], kblk[:, c, :],
                                   B2[:, c, 512 * st4:512 * st4 + 512],
                                   start=(c == 0), stop=(c == 15),
                                   skip_group_check=True)
                inc(mm, "pe")
                wait(v, "v", "pe")
                inc(v.tensor_scalar_add(scr1[0:16, 0:512], psZ[:], EPS_Z), "v")
                wait(a, "a", "v")
                a.activation(out=scr1[0:16, 0:512], in_=scr1[0:16, 0:512],
                             func=AF.Ln)
                rr_done[st4] = inc(a.activation(
                    out=rrow[:, 512 * st4:512 * st4 + 512],
                    in_=scr1[0:16, 0:512], func=AF.Exp, scale=-1.0), "a")
            if "rr" in taps:
                wait(sp, "sp", "a", rr_done[3])
                inc(sp.dma_start(out=tap_t["rr"], in_=rrow[:]), "d_out", 16)

            at_free = [None, None]
            rb_free = [None, None]
            first_at = True
            for st4 in range(4):
                for hh in range(H):
                    sl = hh % 2
                    wait(pe, "pe", "a", rr_done[st4])
                    if rb_free[sl] is not None:
                        wait(pe, "pe", "v", rb_free[sl])
                    pe.matmul(psR[:, sl, :], sel16[:, hh, :],
                              rrow[:, 512 * st4:512 * st4 + 512],
                              start=True, stop=True, skip_group_check=True)
                    if at_free[sl] is not None:
                        wait(pe, "pe", "v", at_free[sl])
                    inc(pe.matmul(psAt[:, sl, :], kv_bf[:, hh, 0:128],
                                  B2[:, hh, 512 * st4:512 * st4 + 512],
                                  start=True, stop=True,
                                  skip_group_check=True), "pe")
                    wait(v, "v", "pe")
                    if first_at:
                        wait(v, "v", "pe", q_end)
                        first_at = False
                    rb_free[sl] = inc(v.tensor_copy(rb_sb[:, sl, :],
                                                    psR[:, sl, :]), "v")
                    at_free[sl] = inc(v.tensor_tensor(
                        B1[:, hh, 512 * st4:512 * st4 + 512],
                        psAt[:, sl, :], rb_sb[:, sl, :], ALU.mult), "v")
            attn_done = cnt("v")
            if "attn" in taps:
                wait(sp, "sp", "v", attn_done)
                inc(sp.dma_start(out=tap_t["attn"],
                                 in_=B1[:].rearrange("p c t -> p (c t)")),
                    "d_out", 16)
            ckpt(3)
            esZ.close()
            esK.close()

            esO = ExitStack()
            psO = esO.enter_context(nc.psum_tensor("psO", [128, 8, 512], F32))
            o_tile = esO.enter_context(nc.sbuf_tensor("o_tile", [128, 2, D], F32))

            wait(g, "g", "pe")
            for hec in range(16):
                inc(g.dma_start(out=B2[:, hec, :],
                                in_=woT_h[128 * hec:128 * hec + 128, :]),
                    "dc", 16)
            wo_ready = cnt("dc")

            o_free = [None, None]
            ot_free = [None, None]
            x1_w = {}
            for tc in range(16):
                ob = tc % 2
                load_x_tile(tc, ob)
                wait(pe, "pe", "dc", wo_ready)
                wait(pe, "pe", "v", attn_done)
                if o_free[ob] is not None:
                    wait(pe, "pe", "a", o_free[ob])
                for gs in range(4):
                    for hec in range(16):
                        pe.matmul(psO[:, 4 * ob + gs, :],
                                  B1[:, hec, 128 * tc:128 * tc + 128],
                                  B2[:, hec, 512 * gs:512 * gs + 512],
                                  start=(hec == 0), stop=False,
                                  skip_group_check=True)
                    inc(pe.matmul(psO[:, 4 * ob + gs, :], ones_bf[0:1, 0:128],
                                  biasB[0:1, 512 * gs:512 * gs + 512],
                                  start=False, stop=True,
                                  skip_group_check=True), "pe")
                wait(a, "a", "pe")
                if ot_free[ob] is not None:
                    wait(a, "a", "d_scr", ot_free[ob])
                o_free[ob] = inc(a.activation(
                    out=o_tile[:, ob, :].rearrange("p (g f) -> p g f", g=4),
                    in_=psO[:, 4 * ob:4 * ob + 4, :], func=AF.Copy), "a")
                wait(v, "v", "a")
                ln_stats(o_tile[:, ob, :])
                v.tensor_scalar(scr1[:, 0:D], o_tile[:, ob, :], cols[:, 4:5],
                                cols[:, 5:6], ALU.mult, ALU.add)
                wait(v, "v", "d_x")
                c1 = inc(v.tensor_add(o_tile[:, ob, :], xt[:, ob, :],
                                      scr1[:, 0:D]), "v")
                x_freed[ob] = ("v", c1)
                wait(sp, "sp", "v", c1)
                x1_w[tc] = inc(sp.dma_start(
                    out=x1_scr[128 * tc:128 * tc + 128, :],
                    in_=o_tile[:, ob, :]), "d_scr", 16)
                ot_free[ob] = x1_w[tc]
            o_end = cnt("pe")
            if "x1" in taps:
                wait(sp, "sp", "d_scr")
                inc(sp.dma_start(out=tap_t["x1"], in_=x1_scr), "d_out", 16)
            ckpt(4)
            esO.close()

            esF = ExitStack()
            w2out = esF.enter_context(nc.sbuf_tensor("w2out", [128, 4, D], F32))
            fcols = esF.enter_context(nc.sbuf_tensor("fcols", [128, 16], F32))

            HAP = lambda j: B2[:, j // 4, 512 * (j % 4):512 * (j % 4) + 512]
            w2out_free = [None] * 4

            for g4 in range(4):
                esF1 = ExitStack()
                psT2_a = esF1.enter_context(nc.psum_tensor(f"psT2a_{g4}", [128, 4, 128], BF16))
                psT2_b = esF1.enter_context(nc.psum_tensor(f"psT2b_{g4}", [128, 4, 128], BF16))
                psT2 = [psT2_a, psT2_b]
                psF = esF1.enter_context(nc.psum_tensor(f"psF_{g4}", [128, 4, 512], F32))
                psX = esF1.enter_context(nc.psum_tensor(f"psX_{g4}", [1, 4, 128], F32))
                for tc in range(4):
                    gt = 4 * g4 + tc
                    b = tc % 2
                    wait(sp, "sp", "d_scr", x1_w[gt])
                    load_x_tile(gt, b, src=x1_scr)
                    wait(v, "v", "d_x")
                    wait(a, "a", "d_x")
                    if g4 == 0 and tc == 0:
                        wait(v, "v", "pe", o_end)
                    if g4 == 0 and tc == 1 and "xtap" in taps:
                        wait(sp, "sp", "d_x")
                        dn = inc(sp.dma_start(out=tap_t["xtap"],
                                              in_=xt[:, b, :]), "d_out", 16)
                        for e_, en_ in ((v, "v"), (a, "a"), (sp, "sp")):
                            wait(e_, en_, "d_out", dn)
                    ln_stats(xt[:, b, :], want_maxmin=True)
                    if g4 == 0 and tc == 1 and "scr1" in taps:
                        wait(sp, "sp", "v")
                        wait(sp, "sp", "a")
                        dn = inc(sp.dma_start(out=tap_t["scr1"], in_=scr1[:]),
                                 "d_out", 16)
                        for e_, en_ in ((v, "v"), (a, "a")):
                            wait(e_, en_, "d_out", dn)
                    if g4 == 0 and tc == 1 and "colsB" in taps:
                        wait(sp, "sp", "v")
                        wait(sp, "sp", "a")
                        dn = inc(sp.dma_start(out=tap_t["colsB"], in_=cols[:]),
                                 "d_out", 16)
                        for e_, en_ in ((v, "v"), (a, "a")):
                            wait(e_, en_, "d_out", dn)
                    vS(v.tensor_tensor(cols[:, 9:10], cols[:, 6:7], cols[:, 2:3],
                                       ALU.subtract))
                    vS(v.tensor_tensor(cols[:, 10:11], cols[:, 2:3], cols[:, 7:8],
                                       ALU.subtract))
                    vS(v.tensor_tensor(cols[:, 9:10], cols[:, 9:10],
                                       cols[:, 10:11], ALU.max))
                    vS(v.tensor_tensor(cols[:, 9:10], cols[:, 9:10],
                                       cols[:, 4:5], ALU.mult))
                    vS(v.tensor_scalar_max(cols[:, 9:10], cols[:, 9:10], 1e-5))
                    if g4 == 0 and tc == 1 and "colsA" in taps:
                        wait(sp, "sp", "v")
                        dn = inc(sp.dma_start(out=tap_t["colsA"], in_=cols[:]),
                                 "d_out", 16)
                        for e_, en_ in ((v, "v"), (a, "a")):
                            wait(e_, en_, "d_out", dn)
                    wait(v, "v", "v_wsc")
                    vS(v.tensor_tensor(cols[:, 16 + tc:17 + tc], cols[:, 9:10],
                                       wsc[:, 1:2], ALU.mult))
                    vS(v.reciprocal(cols[:, 10:11], cols[:, 9:10]))
                    vS(v.tensor_scalar_mul(cols[:, 10:11], cols[:, 10:11], 127.0))
                    vS(v.tensor_tensor(cols[:, 11:12], cols[:, 4:5],
                                       cols[:, 10:11], ALU.mult))
                    inc(v.tensor_tensor(cols[:, 8:9], cols[:, 5:6],
                                        cols[:, 10:11], ALU.mult), "v")
                    wait(v, "v", "v")
                    c1 = inc(v.tensor_scalar(scr1[:, 0:D], xt[:, b, :],
                                             cols[:, 11:12], cols[:, 8:9],
                                             ALU.mult, ALU.add), "v")
                    x_freed[b] = ("v", c1)
                    for hf in range(2):
                        v.tensor_copy(scri[:], scr1[:, 1024 * hf:1024 * hf + 1024])
                        v.tensor_scalar_mul(B1[:, 12 + 2 * hf, 1536:2048],
                                            scri[:, 0:512],
                                            cols[:, 16 + tc:17 + tc])
                        inc(v.tensor_scalar_mul(B1[:, 13 + 2 * hf, 1536:2048],
                                                scri[:, 512:1024],
                                                cols[:, 16 + tc:17 + tc]), "v")
                    wait(pe, "pe", "v", cnt("v"))
                    for q in range(4):
                        pj = q % 2
                        if tr_done[pj] is not None:
                            wait(pe, "pe", "v_tr", tr_done[pj])
                        for m in range(4):
                            inc(pe.transpose(
                                psT2[pj][:, m, :],
                                B1[:, 12 + q, 1536 + 128 * m:1536 + 128 * m + 128],
                                ident_bf[:]), "pe_tr")
                        wait(v, "v", "pe_tr")
                        tr_done[pj] = inc(v.tensor_copy(
                            B1[:, 4 * q:4 * q + 4, 128 * tc:128 * tc + 128],
                            psT2[pj][:]), "v_tr")
                if g4 == 0 and "n1f" in taps:
                    wait(sp, "sp", "v_tr")
                    dn = inc(sp.dma_start(out=tap_t["n1f"],
                             in_=B1[:].rearrange("p c t -> p (c t)")), "d_out", 16)
                    for e_, en_ in ((v, "v"), (a, "a"), (pe, "pe"), (g, "g"),
                                    (sp, "sp")):
                        wait(e_, en_, "d_out", dn)
                if g4 == 0:
                    ckpt(5)
                    wait(sp, "sp", "d_wqo", w_all)

                w1_used = [None, None]

                def load_w1_window(w, b):
                    if w1_used[b] is not None:
                        wait(sp, "sp", "pe", w1_used[b])
                    inc(sp.dma_start(
                        out=B1[:, :, 512 + 512 * b:1024 + 512 * b],
                        in_=w1q_scr[w].rearrange("c p f -> p c f")), "d_w1", 16)
                    return cnt("d_w1")

                w1_ready = [load_w1_window(0, 0), None]
                hs_free = [None] * 4
                first_h = (g4 == 0)
                wait(pe, "pe", "a")
                for w in range(16):
                    b = w % 2
                    if w + 1 < 16:
                        w1_ready[(w + 1) % 2] = load_w1_window(w + 1, (w + 1) % 2)
                    wait(pe, "pe", "d_w1", w1_ready[b])
                    wait(pe, "pe", "v_tr")
                    for fb in range(4):
                        j = 4 * w + fb
                        if hs_free[fb] is not None:
                            wait(pe, "pe", "a", hs_free[fb])
                        for dc in range(16):
                            mm = pe.matmul(
                                psF[:, fb, :],
                                B1[:, dc, 512 + 512 * b + 128 * fb:
                                   512 + 512 * b + 128 * fb + 128],
                                B1[:, dc, 0:512],
                                start=(dc == 0), stop=(dc == 15),
                                skip_group_check=True)
                        inc(mm, "pe")
                        if fb == 3:
                            w1_used[b] = cnt("pe")
                        wait(a, "a", "pe")
                        if first_h:
                            wait(a, "a", "pe", o_end)
                            first_h = False
                        hs_free[fb] = inc(a.activation(
                            out=HAP(j), in_=psF[:, fb, :], func=AF.Silu), "a")
                if g4 == 0 and "hs0" in taps:
                    wait(sp, "sp", "a")
                    dn = inc(sp.dma_start(out=tap_t["hs0"],
                             in_=B2[:].rearrange("p c t -> p (c t)")), "d_out", 16)
                    for e_, en_ in ((v, "v"), (a, "a"), (pe, "pe"), (g, "g"),
                                    (sp, "sp")):
                        wait(e_, en_, "d_out", dn)
                if g4 == 0:
                    ckpt(6)
                f2_end_a = cnt("a")
                mxV, mnV = B1[:, 14, 512:1024], B1[:, 14, 1024:1536]
                wait(v, "v", "a", f2_end_a)
                v.tensor_copy(mxV, HAP(0))
                v.tensor_copy(mnV, HAP(0))
                for j in range(1, 64):
                    v.tensor_tensor(mxV, mxV, HAP(j), ALU.max)
                    v.tensor_tensor(mnV, mnV, HAP(j), ALU.min)
                v.tensor_scalar_mul(mnV, mnV, -1.0)
                inc(v.tensor_tensor(mxV, mxV, mnV, ALU.max), "v")
                for tc in range(4):
                    wait(pe, "pe", "v")
                    inc(pe.transpose(psT2[tc % 2][:, 0, :],
                                     B1[:, 14, 512 + 128 * tc:512 + 128 * tc + 128],
                                     ident_bf[:]), "pe_tr")
                    wait(v, "v", "pe_tr")
                    vS(v.tensor_reduce(fcols[:, tc:tc + 1], psT2[tc % 2][:, 0, :],
                                       AX.X, ALU.max))
                vS(v.tensor_scalar_max(fcols[:, 0:4], fcols[:, 0:4], 1e-5))
                for tc in range(4):
                    vS(v.tensor_tensor(fcols[:, 12 + tc:13 + tc],
                                       fcols[:, tc:tc + 1], wsc[:, 3:4], ALU.mult))
                vS(v.reciprocal(fcols[:, 4:8], fcols[:, 0:4]))
                inc(v.tensor_scalar_mul(fcols[:, 4:8], fcols[:, 4:8], 127.0), "v")
                wait(v, "v", "v")
                wait(pe, "pe", "v")
                for tc in range(4):
                    inc(pe.transpose(psX[0:1, tc, :], fcols[:, 4 + tc:5 + tc],
                                     ident_f32[:]), "pe")
                wait(v, "v", "pe")
                inc(v.tensor_copy(scrE3[0:1, 1, :],
                                  psX[0:1, :, :].rearrange("o t c -> o (t c)")),
                    "v")
                wait(pe, "pe", "v")
                wait(pe, "pe", "a")
                inc(pe.matmul(psF[:, 0, :], ones_f32[0:1, :], scrE3[0:1, 1, :],
                              start=True, stop=True, skip_group_check=True), "pe")
                wait(v, "v", "pe")
                inc(v.tensor_copy(scrE3[:, 0, :], psF[:, 0, :]), "v_s2")
                wait(g, "g", "v_s2")
                wait(v, "v", "v_s2")
                for j in range(64):
                    if j % 2 == 0:
                        v.tensor_tensor(scrE1[:], HAP(j), scrE3[:, 0, :], ALU.mult)
                        v.tensor_copy(scri[:, 0:512], scrE1[:])
                        inc(v.tensor_copy(HAP(j), scri[:, 0:512]), "v_n2")
                    else:
                        g.tensor_tensor(scrE2[:], HAP(j), scrE3[:, 0, :], ALU.mult)
                        g.tensor_copy(scri[:, 512:1024], scrE2[:])
                        inc(g.tensor_copy(HAP(j), scri[:, 512:1024]), "g_n2")
                if g4 == 0 and "h0" in taps:
                    wait(sp, "sp", "v_n2")
                    wait(sp, "sp", "g_n2")
                    inc(sp.dma_start(out=tap_t["h0"],
                                     in_=B2[:].rearrange("p c t -> p (c t)")),
                        "d_out", 16)
                if g4 == 0:
                    ckpt(7)
                esF1.close()
                esF2 = ExitStack()
                psW2 = esF2.enter_context(nc.psum_tensor(f"psW2_{g4}", [128, 8, 512], F32))
                wait(pe, "pe", "v_n2")
                wait(pe, "pe", "g_n2")

                def load_w2(gh, fc, b):
                    inc(sp.dma_start(out=B1[:, 8 + 2 * b, 1536:2048],
                                     in_=w2q_scr[fc, :, 1024 * gh:1024 * gh + 512]),
                        "d_w2", 16)
                    inc(sp.dma_start(out=B1[:, 9 + 2 * b, 1536:2048],
                                     in_=w2q_scr[fc, :,
                                                 1024 * gh + 512:1024 * gh + 1024]),
                        "d_w2", 16)
                    return cnt("d_w2")

                w2_used = [None, None]
                w2_ready = [load_w2(0, 0, 0), None]
                gh_evac = None
                for gh in range(2):
                    for fc in range(64):
                        b = fc % 2
                        nb = (fc + 1) % 2
                        if fc + 1 < 64 or gh == 0:
                            if w2_used[nb] is not None:
                                wait(sp, "sp", "pe", w2_used[nb])
                            w2_ready[nb] = (load_w2(gh, fc + 1, nb)
                                            if fc + 1 < 64 else load_w2(1, 0, nb))
                        wait(pe, "pe", "d_w2", w2_ready[b])
                        if fc == 0 and gh == 1 and gh_evac is not None:
                            wait(pe, "pe", "a", gh_evac)
                        for tc in range(4):
                            for gsub in range(2):
                                mm = pe.matmul(
                                    psW2[:, 2 * tc + gsub, :],
                                    B2[:, fc // 4,
                                       512 * (fc % 4) + 128 * tc:
                                       512 * (fc % 4) + 128 * tc + 128],
                                    B1[:, 8 + 2 * b + gsub, 1536:2048],
                                    start=(fc == 0), stop=(fc == 63),
                                    skip_group_check=True)
                        w2_used[b] = inc(mm, "pe")
                    wait(a, "a", "pe")
                    for tc in range(4):
                        if gh == 0 and w2out_free[tc] is not None:
                            wait(a, "a", "d_out", w2out_free[tc])
                        gh_evac = inc(a.activation(
                            out=w2out[:, tc, 1024 * gh:1024 * gh + 1024]
                            .rearrange("p (s f) -> p s f", s=2),
                            in_=psW2[:, 2 * tc:2 * tc + 2, :],
                            func=AF.Copy, scale=fcols[:, 12 + tc:13 + tc]), "a")
                if g4 == 0:
                    ckpt(8)
                esF2.close()
                for tc in range(4):
                    gt = 4 * g4 + tc
                    b = tc % 2
                    load_x_tile(gt, b, src=x1_scr)
                    wait(v, "v", "a", gh_evac)
                    ln_stats(w2out[:, tc, :])
                    v.tensor_scalar(scr1[:, 0:D], w2out[:, tc, :], cols[:, 4:5],
                                    cols[:, 5:6], ALU.mult, ALU.add)
                    wait(v, "v", "d_x")
                    c1 = inc(v.tensor_add(w2out[:, tc, :], xt[:, b, :],
                                          scr1[:, 0:D]), "v")
                    x_freed[b] = ("v", c1)
                    wait(sp, "sp", "v", c1)
                    w2out_free[tc] = inc(sp.dma_start(
                        out=out_h[128 * gt:128 * gt + 128, :],
                        in_=w2out[:, tc, :]), "d_out", 16)
            wait(sp, "sp", "d_out")
            esF.close()
    except StopEmission:
        pass
    return nc


def kernel(L0_attn_hidden_states, L0_attn_q_w, L0_attn_q_b, L0_attn_k_w,
           L0_attn_k_b, L0_attn_v_w, L0_attn_v_b, L0_attn_o_w, L0_attn_o_b,
           L0_ffn_W1, L0_ffn_W2):
    x = np.asarray(L0_attn_hidden_states, dtype=np.float32)
    ct = lambda w: np.ascontiguousarray(np.asarray(w, dtype=np.float32).T)
    wqT, wkT, wvT, woT = (ct(L0_attn_q_w), ct(L0_attn_k_w),
                          ct(L0_attn_v_w), ct(L0_attn_o_w))
    w1T, w2T = ct(L0_ffn_W1), ct(L0_ffn_W2)
    row = lambda b: np.ascontiguousarray(
        np.asarray(b, dtype=np.float32).reshape(1, D))
    bq, bk, bv, bo = (row(L0_attn_q_b), row(L0_attn_k_b),
                      row(L0_attn_v_b), row(L0_attn_o_b))

    nc = build_program()
    in_maps = []
    for c in range(NCORES):
        b, hp = c // 2, c % 2
        xs = x[b] if hp == 0 else np.ascontiguousarray(
            np.concatenate([x[b, T:], x[b, :T]], axis=0))
        in_maps.append({
            "x": np.ascontiguousarray(xs), "wqT": wqT, "wkT": wkT, "wvT": wvT,
            "woT": woT, "w1T": w1T, "w2T": w2T,
            "bq": bq, "bk": bk, "bv": bv, "bo": bo,
        })
    res = run_bass_kernel_spmd(nc, in_maps, core_ids=list(range(NCORES)))
    out = np.empty((4, S, D), dtype=np.float32)
    for c in range(NCORES):
        b, hp = c // 2, c % 2
        out[b, T * hp:T * hp + T, :] = np.asarray(res.results[c]["out"])
    return out


# version 11
# speedup vs baseline: 1.6742x; 1.6742x over previous
import numpy as np
from contextlib import ExitStack

import concourse.bass as bass
import concourse.mybir as mybir
from concourse.bass_utils import run_bass_kernel_spmd

F32 = mybir.dt.float32
BF16 = mybir.dt.bfloat16
I32 = mybir.dt.int32
AF = mybir.ActivationFunctionType
ALU = mybir.AluOpType
AX = mybir.AxisListType

D = 2048
DFF = 8192
T = 2048
S = 4096
H = 16
NT = 16
NTS = 32
EPS_LN = 1e-5
EPS_Z = 1e-6
QCH = 512
NCORES = 8


class StopEmission(Exception):
    pass


def build_program(taps=(), limit=99, no_quant=False):
    taps = set(taps)
    nc = bass.Bass("TRN2", target_bir_lowering=False)

    x_h = nc.dram_tensor("x", [S, D], F32, kind="ExternalInput").ap()
    wqT_h = nc.dram_tensor("wqT", [D, D], F32, kind="ExternalInput").ap()
    wkT_h = nc.dram_tensor("wkT", [D, D], F32, kind="ExternalInput").ap()
    wvT_h = nc.dram_tensor("wvT", [D, D], F32, kind="ExternalInput").ap()
    woT_h = nc.dram_tensor("woT", [D, D], F32, kind="ExternalInput").ap()
    w1T_h = nc.dram_tensor("w1T", [D, DFF], F32, kind="ExternalInput").ap()
    w2T_h = nc.dram_tensor("w2T", [DFF, D], F32, kind="ExternalInput").ap()
    bq_h = nc.dram_tensor("bq", [1, D], F32, kind="ExternalInput").ap()
    bk_h = nc.dram_tensor("bk", [1, D], F32, kind="ExternalInput").ap()
    bv_h = nc.dram_tensor("bv", [1, D], F32, kind="ExternalInput").ap()
    bo_h = nc.dram_tensor("bo", [1, D], F32, kind="ExternalInput").ap()
    out_h = nc.dram_tensor("out", [T, D], F32, kind="ExternalOutput").ap()
    x1_scr = nc.dram_tensor("x1_scr", [T, D], F32).ap()
    w1q_scr = nc.dram_tensor("w1q_scr", [16, 16, 128, 512], BF16).ap()
    w2q_scr = nc.dram_tensor("w2q_scr", [64, 128, D], BF16).ap()
    tap_t = {}
    for tname, shape in [("xn", [128, 16 * T]), ("kv", [128, H * 132]),
                         ("qe", [128, 16 * T]), ("attn", [128, 16 * T]),
                         ("h0", [128, 16 * T]), ("rr", [16, T]),
                         ("n1f", [128, 16 * T]), ("hs0", [128, 16 * T])]:
        if tname in taps:
            tap_t[tname] = nc.dram_tensor("tap_" + tname, shape, BF16,
                                          kind="ExternalOutput").ap()
    if "x1" in taps:
        tap_t["x1"] = nc.dram_tensor("tap_x1", [T, D], F32,
                                     kind="ExternalOutput").ap()
    if "wsc" in taps:
        tap_t["wsc"] = nc.dram_tensor("tap_wsc", [128, 8], F32,
                                      kind="ExternalOutput").ap()
    if "colsA" in taps:
        tap_t["colsA"] = nc.dram_tensor("tap_colsA", [128, 64], F32,
                                        kind="ExternalOutput").ap()
    if "colsB" in taps:
        tap_t["colsB"] = nc.dram_tensor("tap_colsB", [128, 64], F32,
                                        kind="ExternalOutput").ap()
    if "scr1" in taps:
        tap_t["scr1"] = nc.dram_tensor("tap_scr1", [128, D], F32,
                                       kind="ExternalOutput").ap()
    if "xtap" in taps:
        tap_t["xtap"] = nc.dram_tensor("tap_xtap", [128, D], F32,
                                       kind="ExternalOutput").ap()

    es = ExitStack()
    try:
      with es:
        sbuf = lambda *aa: es.enter_context(nc.sbuf_tensor(*aa))

        ident_bf = sbuf("ident_bf", [128, 128], BF16)
        ident_f32 = sbuf("ident_f32", [128, 128], F32)
        ones_bf = sbuf("ones_bf", [128, 512], BF16)
        ones_f32 = sbuf("ones_f32", [128, 128], F32)
        ones_col = sbuf("ones_col", [128, 1], F32)
        biasA = sbuf("biasA", [128, D], BF16)
        biasB = sbuf("biasB", [128, D], BF16)
        scr1 = sbuf("scr1", [128, D], F32)
        scrE1 = sbuf("scrE1", [128, 512], F32)
        scrE2 = sbuf("scrE2", [128, 512], F32)
        scrE3 = sbuf("scrE3", [128, 2, 512], F32)
        scri = sbuf("scri", [128, 1024], I32)
        cols = sbuf("cols", [128, 64], F32)
        qrow = sbuf("qrow", [16, 8], F32)
        wsc = sbuf("wsc", [128, 8], F32)
        abscol = sbuf("abscol", [128, 4], F32)
        xt = sbuf("xt", [128, 2, D], F32)
        B1 = sbuf("B1", [128, 16, T], BF16)
        B2 = sbuf("B2", [128, 16, T], BF16)

        C = {"sems": {}, "cnt": {}, "waited": {}}

        def sem(name):
            if name not in C["sems"]:
                C["sems"][name] = es.enter_context(nc.semaphore(name))
                C["cnt"][name] = 0
            return C["sems"][name]

        def inc(instr, name, k=1):
            instr.then_inc(sem(name), k)
            C["cnt"][name] += k
            return C["cnt"][name]

        def cnt(name):
            return C["cnt"].get(name, 0)

        def wait(eng, ename, name, val=None):
            if val is None:
                val = cnt(name)
            if val is None or val <= 0:
                return
            key = (ename, name)
            if C["waited"].get(key, -1) >= val:
                return
            eng.wait_ge(sem(name), val)
            C["waited"][key] = val

        if True:
            sp, g, pe, v, a = nc.sync, nc.gpsimd, nc.tensor, nc.vector, nc.scalar

            def gS(instr):
                c = inc(instr, "g_f2")
                wait(g, "g", "g_f2", c)
                return c

            def vS(instr):
                c = inc(instr, "v")
                wait(v, "v", "v", c)
                return c

            def aS(instr):
                c = inc(instr, "a")
                wait(a, "a", "a", c)
                return c

            def ckpt(n):
                if limit <= n:
                    inc(sp.dma_start(out=out_h[0:128, :], in_=scr1[:, 0:D]),
                        "d_out", 16)
                    wait(sp, "sp", "d_out")
                    raise StopEmission()

            _CKPT = ckpt
            g.memset(ident_bf[:], 0.0)
            g.affine_select(out=ident_bf[:], in_=ident_bf[:],
                            compare_op=ALU.not_equal, fill=1.0, base=0,
                            pattern=[[-1, 128]], channel_multiplier=1)
            g.memset(ident_f32[:], 0.0)
            g.affine_select(out=ident_f32[:], in_=ident_f32[:],
                            compare_op=ALU.not_equal, fill=1.0, base=0,
                            pattern=[[-1, 128]], channel_multiplier=1)
            ckpt(0.005)
            g.memset(ones_bf[:], 1.0)
            g.memset(ones_f32[:], 1.0)
            g.memset(ones_col[:], 1.0)
            ckpt(0.01)
            g.memset(biasA[:], 0.0)
            inc(g.memset(biasB[:], 0.0), "g")
            ckpt(0.02)
            inc(g.dma_start(out=biasA[0:1, :], in_=bq_h), "dc", 16)
            ckpt(0.03)
            inc(g.dma_start(out=biasA[32:33, :], in_=bk_h), "dc", 16)
            inc(g.dma_start(out=biasA[64:65, :], in_=bv_h), "dc", 16)
            inc(g.dma_start(out=biasB[0:1, :], in_=bo_h), "dc", 16)
            ckpt(0.05)

            def ln_stats(src_ap, want_maxmin=False, width=D):
                v.tensor_reduce(cols[:, 0:1], src_ap, AX.X, ALU.add)
                if want_maxmin:
                    v.tensor_reduce(cols[:, 6:7], src_ap, AX.X, ALU.max)
                    v.tensor_reduce(cols[:, 7:8], src_ap, AX.X, ALU.min)
                wait(a, "a", "v")
                inc(a.activation(out=scr1[:, 0:width], in_=src_ap,
                                 func=AF.Square), "a")
                wait(v, "v", "a")
                vS(v.tensor_reduce(cols[:, 1:2], scr1[:, 0:width], AX.X, ALU.add))
                ninv = 1.0 / float(width)
                vS(v.tensor_scalar_mul(cols[:, 2:3], cols[:, 0:1], ninv))
                vS(v.tensor_scalar_mul(cols[:, 3:4], cols[:, 1:2], ninv))
                vS(v.tensor_tensor(cols[:, 8:9], cols[:, 2:3], cols[:, 2:3],
                                   ALU.mult))
                vS(v.tensor_tensor(cols[:, 3:4], cols[:, 3:4], cols[:, 8:9],
                                   ALU.subtract))
                inc(v.tensor_scalar_add(cols[:, 3:4], cols[:, 3:4], EPS_LN), "v")
                wait(a, "a", "v")
                aS(a.activation(out=cols[:, 3:4], in_=cols[:, 3:4], func=AF.Ln))
                inc(a.activation(out=cols[:, 4:5], in_=cols[:, 3:4],
                                 func=AF.Exp, scale=-0.5), "a")
                wait(v, "v", "a")
                vS(v.tensor_scalar_mul(cols[:, 5:6], cols[:, 2:3], -1.0))
                inc(v.tensor_tensor(cols[:, 5:6], cols[:, 5:6], cols[:, 4:5],
                                    ALU.mult), "v")
                wait(v, "v", "v")

            def elu_evac(ps_ap, dst_ap, par=[0]):
                p = par[0] = 1 - par[0]
                e3 = scrE3[:, p, :]
                c_ps = inc(v.tensor_copy(e3, ps_ap), "v")
                v.tensor_scalar_min(scrE1[:], e3, 0.0)
                inc(v.tensor_scalar_max(scrE2[:], e3, 0.0), "v")
                wait(a, "a", "v")
                inc(a.activation(out=scrE1[:], in_=scrE1[:], func=AF.Exp), "a")
                wait(v, "v", "a")
                c_add = inc(v.tensor_add(dst_ap, scrE1[:], scrE2[:]), "v")
                return c_ps, c_add

            def gen_abssum(wq_in):
                sub = 0
                freed = [None, None]
                for mi, (wh, nch, fw) in enumerate([(w1T_h, 16, DFF),
                                                    (w2T_h, 64, D)]):
                    first = True
                    for ch in range(nch):
                        for sc in range(fw // QCH):
                            b = sub % 2
                            if freed[b] is not None:
                                wait(sp, "sp", "g_abs", freed[b])
                            inc(sp.dma_start(
                                out=wq_in[:, b, :],
                                in_=wh[128 * ch:128 * ch + 128,
                                       QCH * sc:QCH * sc + QCH]), "d_wabs", 16)
                            wait(v, "v", "d_wabs")
                            c0 = inc(v.tensor_reduce(
                                abscol[:, b:b + 1], wq_in[:, b, :], AX.X,
                                ALU.add, apply_absolute_value=True), "g_abs")
                            freed[b] = c0
                            wait(v, "v", "g_abs", c0)
                            if first:
                                inc(v.tensor_copy(cols[:, 32 + mi:33 + mi],
                                                  abscol[:, b:b + 1]), "g_abs")
                                first = False
                            else:
                                inc(v.tensor_tensor(
                                    cols[:, 32 + mi:33 + mi],
                                    cols[:, 32 + mi:33 + mi],
                                    abscol[:, b:b + 1], ALU.add), "g_abs")
                            wait(v, "v", "g_abs")
                            sub += 1
                            yield

            def quant_finalize(psSm):
                wait(pe, "pe", "g_abs")
                inc(pe.matmul(psSm[0:1, 0:2], ones_col[:], cols[:, 32:34],
                              start=True, stop=True, skip_group_check=True), "pe")
                wait(v, "v", "pe")
                vS(v.tensor_scalar(qrow[0:1, 0:2], psSm[0:1, 0:2],
                                   1.0 / float(DFF * D), 1e-5, ALU.mult, ALU.max))
                vS(v.reciprocal(qrow[0:1, 2:4], qrow[0:1, 0:2]))
                inc(v.tensor_scalar_mul(qrow[0:1, 4:6], qrow[0:1, 0:2],
                                        1.0 / 127.0), "v")
                wait(pe, "pe", "v")
                pe.matmul(psSm[:, 0:1], ones_f32[0:1, :], qrow[0:1, 2:3],
                          start=True, stop=True, skip_group_check=True)
                pe.matmul(psSm[:, 1:2], ones_f32[0:1, :], qrow[0:1, 4:5],
                          start=False, stop=True, skip_group_check=True)
                pe.matmul(psSm[:, 2:3], ones_f32[0:1, :], qrow[0:1, 3:4],
                          start=False, stop=True, skip_group_check=True)
                inc(pe.matmul(psSm[:, 3:4], ones_f32[0:1, :], qrow[0:1, 5:6],
                              start=False, stop=True, skip_group_check=True), "pe")
                wait(v, "v", "pe")
                inc(v.tensor_copy(wsc[:, 0:4], psSm[:, 0:4]), "v_wsc")
                if "wsc" in taps:
                    wait(sp, "sp", "v_wsc")
                    inc(sp.dma_start(out=tap_t["wsc"], in_=wsc[:]), "d_out", 16)

            W1DONE = [None]

            def gen_round(wq_in, wq_bfb, qscr):
                sub = 0
                freed = [None, None]
                out_d = [None, None]
                for (wh, scr_t, nch, fw, sci) in [(w1T_h, w1q_scr, 16, DFF, 0),
                                                  (w2T_h, w2q_scr, 64, D, 2)]:
                    for ch in range(nch):
                        for sc in range(fw // QCH):
                            b = sub % 2
                            if freed[b] is not None:
                                wait(sp, "sp", "v_q", freed[b])
                            inc(sp.dma_start(
                                out=wq_in[:, b, :],
                                in_=wh[128 * ch:128 * ch + 128,
                                       QCH * sc:QCH * sc + QCH]), "d_wrnd", 16)
                            wait(a, "a", "d_wrnd")
                            wait(a, "a", "v_wsc")
                            wait(a, "a", "v_q")
                            inc(a.activation(out=qscr[:], in_=wq_in[:, b, :],
                                             func=AF.Copy,
                                             scale=wsc[:, sci:sci + 1]), "a_q")
                            wait(v, "v", "a_q")
                            v.tensor_scalar_min(qscr[:], qscr[:], 1.0)
                            v.tensor_scalar_max(scri[:, 0:QCH], qscr[:], -1.0)
                            if out_d[b] is not None:
                                wait(v, "v", "d_wqo", out_d[b])
                            freed[b] = inc(v.tensor_copy(wq_bfb[:, b, :],
                                                         scri[:, 0:QCH]), "v_q")
                            wait(sp, "sp", "v_q")
                            if scr_t is w1q_scr:
                                dst = scr_t[sc, ch, :, :]
                            else:
                                dst = scr_t[ch, :, QCH * sc:QCH * sc + QCH]
                            out_d[b] = inc(sp.dma_start(
                                out=dst, in_=wq_bfb[:, b, :]), "d_wqo", 16)
                            sub += 1
                            yield
                    if scr_t is w1q_scr:
                        W1DONE[0] = cnt("d_wqo")

            def drive(gg, n):
                for _ in range(n):
                    try:
                        next(gg)
                    except StopIteration:
                        return

            def drain(gg):
                for _ in gg:
                    pass

            esK = ExitStack()
            kv_bf = esK.enter_context(nc.sbuf_tensor("kv_bf", [128, H, 132], BF16))
            kblk = esK.enter_context(nc.sbuf_tensor("kblk", [128, H, 16], BF16))
            sel16 = esK.enter_context(nc.sbuf_tensor("sel16", [16, 16, 128], BF16))
            rrow = esK.enter_context(nc.sbuf_tensor("rrow", [16, T], BF16))
            wq_in = esK.enter_context(nc.sbuf_tensor("wq_in", [128, 2, QCH], F32))
            wq_bfb = esK.enter_context(nc.sbuf_tensor("wq_bfb", [128, 2, QCH], BF16))
            qscr = esK.enter_context(nc.sbuf_tensor("qscr", [128, QCH], F32))

            g.memset(kblk[:], 0.0)
            g.memset(sel16[:], 0.0)
            inc(g.affine_select(out=sel16[:], in_=sel16[:],
                                compare_op=ALU.not_equal, fill=1.0, base=0,
                                pattern=[[-1, 16], [0, 128]],
                                channel_multiplier=1), "g")
            ckpt(0.1)

            esA = ExitStack()
            kp_bf = esA.enter_context(nc.sbuf_tensor("kp_bf", [128, 1024], BF16))
            vext = esA.enter_context(nc.sbuf_tensor("vext", [128, 8, 132], BF16))
            xn_tm = esA.enter_context(nc.sbuf_tensor("xn_tm", [128, D], BF16))
            xn_oth = esA.enter_context(nc.sbuf_tensor("xn_oth", [128, 16, 128], BF16))
            psProj = esA.enter_context(nc.psum_tensor("psProj", [128, 2, 512], F32))
            psKV = [esA.enter_context(nc.psum_tensor(f"psKV{i}", [128, 2, 132], F32))
                    for i in range(4)]
            psT_a = esA.enter_context(nc.psum_tensor("psT_a", [128, 4, 128], BF16))
            psT_b = esA.enter_context(nc.psum_tensor("psT_b", [128, 4, 128], BF16))
            psT = [psT_a, psT_b]
            psSm = psProj[:, 0, 0:8]

            inc(g.memset(vext[:], 1.0), "g")

            def load_wkv_pair(p):
                for dc in range(16):
                    inc(g.dma_start(
                        out=B2[:, dc, 0:1024],
                        in_=wkT_h[128 * dc:128 * dc + 128,
                                  1024 * p:1024 * p + 1024]), "dc", 16)
                for dc in range(16):
                    inc(g.dma_start(
                        out=B2[:, dc, 1024:2048],
                        in_=wvT_h[128 * dc:128 * dc + 128,
                                  1024 * p:1024 * p + 1024]), "dc", 16)
                return cnt("dc")

            wkv_ready = load_wkv_pair(0)
            gen_a = gen_abssum(wq_in)
            gen_r = None

            x_freed = [None, None]
            tr_done = [None, None]
            pk_free = None
            pv_free = None
            xn_tm_free = [None]

            def load_x_tile(row, b, src=x_h, tag="d_x"):
                if x_freed[b] is not None:
                    wait(sp, "sp", x_freed[b][0], x_freed[b][1])
                inc(sp.dma_start(out=xt[:, b, :],
                                 in_=src[128 * row:128 * row + 128, :]), tag, 16)

            def build_xn(i, b, dbg=False):
                if dbg:
                    ckpt(0.15)
                wait(v, "v", "d_x")
                wait(a, "a", "d_x")
                ln_stats(xt[:, b, :])
                if dbg:
                    ckpt(0.18)
                if xn_tm_free[0] is not None:
                    wait(v, "v", "pe_tr", xn_tm_free[0])
                c1 = inc(v.tensor_scalar(xn_tm[:], xt[:, b, :], cols[:, 4:5],
                                         cols[:, 5:6], ALU.mult, ALU.add), "v")
                x_freed[b] = ("v", c1)
                if dbg:
                    ckpt(0.19)
                wait(pe, "pe", "v", c1)
                for q in range(4):
                    pj = q % 2
                    if tr_done[pj] is not None:
                        wait(pe, "pe", "v_tr", tr_done[pj])
                    for m in range(4):
                        j = 4 * q + m
                        inc(pe.transpose(psT[pj][:, m, :],
                                         xn_tm[:, 128 * j:128 * j + 128],
                                         ident_bf[:]), "pe_tr")
                    wait(v, "v", "pe_tr")
                    dst = (B1[:, 4 * q:4 * q + 4, 128 * i:128 * i + 128] if i < NT
                           else xn_oth[:, 4 * q:4 * q + 4, :])
                    tr_done[pj] = inc(v.tensor_copy(dst, psT[pj][:]), "v_tr")
                xn_tm_free[0] = cnt("pe_tr")

            for p in range(2):
                if p == 1:
                    if no_quant:
                        ckpt(0.9)
                    quant_finalize(psSm)
                    wait(pe, "pe", "v_wsc")
                    wait(g, "g", "pe")
                    wkv_ready = load_wkv_pair(1)
                    gen_r = gen_round(wq_in, wq_bfb, qscr)
                    wait(pe, "pe", "v_kv")
                for i in range(NTS):
                    b = i % 2
                    own = i < NT
                    if p == 0 or not own:
                        load_x_tile(i, b)
                        build_xn(i, b, dbg=(p == 0 and i == 0))
                    if p == 0 and i == 0:
                        ckpt(0.2)
                    if p == 0 and i == 2:
                        ckpt(0.36)
                    pass
                    xnsrc = ((lambda dc, i=i: B1[:, dc, 128 * i:128 * i + 128])
                             if own else (lambda dc: xn_oth[:, dc, :]))
                    wait(pe, "pe", "dc", wkv_ready)
                    wait(pe, "pe", "v_tr")
                    for s2 in range(2):
                        strip = 2 * p + s2
                        if pk_free is not None:
                            wait(pe, "pe", "v", pk_free[0])
                            if pk_free[1] is not None:
                                wait(pe, "pe", "a", pk_free[1])
                        for dc in range(16):
                            pe.matmul(psProj[:, 0, :], xnsrc(dc),
                                      B2[:, dc, 512 * s2:512 * s2 + 512],
                                      start=(dc == 0), stop=False,
                                      skip_group_check=True)
                        inc(pe.matmul(psProj[:, 0, :], ones_bf[32:33, 0:128],
                                      biasA[32:33, 512 * strip:512 * strip + 512],
                                      start=False, stop=True,
                                      skip_group_check=True), "pe")
                        wait(v, "v", "pe")
                        cps, kp_add = elu_evac(psProj[:, 0, :],
                                               kp_bf[:, 512 * s2:512 * s2 + 512])
                        pk_free = (cps, None)
                        if pv_free is not None:
                            wait(pe, "pe", "a", pv_free)
                        for dc in range(16):
                            pe.matmul(psProj[:, 1, :], xnsrc(dc),
                                      B2[:, dc, 1024 + 512 * s2:1024 + 512 * s2 + 512],
                                      start=(dc == 0), stop=False,
                                      skip_group_check=True)
                        inc(pe.matmul(psProj[:, 1, :], ones_bf[64:65, 0:128],
                                      biasA[64:65, 512 * strip:512 * strip + 512],
                                      start=False, stop=True,
                                      skip_group_check=True), "pe")
                        wait(a, "a", "pe")
                        pv_free = inc(a.activation(
                            out=vext[:, 4 * s2:4 * s2 + 4, 0:128],
                            in_=psProj[:, 1, :].rearrange("p (h f) -> p h f", h=4),
                            func=AF.Copy), "a")
                        wait(pe, "pe", "v", kp_add)
                        wait(pe, "pe", "a", pv_free)
                        for hh in range(4):
                            pst = psKV[2 * s2 + hh // 2]
                            st = (i == 0 and hh % 2 == 0)
                            mm = pe.matmul(
                                pst[:, hh % 2, 0:129],
                                kp_bf[:, 512 * s2 + 128 * hh:
                                      512 * s2 + 128 * hh + 128],
                                vext[:, 4 * s2 + hh, 0:129],
                                start=st, stop=(i == NTS - 1),
                                skip_group_check=True)
                            if hh == 3:
                                inc(mm, "pe")
                            if hh == 3 and p == 0 and not no_quant:
                                drive(gen_a, 8)
                                if i == NTS - 1 and s2 == 1:
                                    drain(gen_a)
                    if p == 0 and i in (0, 1, 2, 3, 15, 17):
                        ckpt({0: 0.3, 1: 0.35, 2: 0.38, 3: 0.4, 15: 0.5, 17: 0.55}[i])
                wait(v, "v", "pe")
                for s2 in range(2):
                    strip = 2 * p + s2
                    for hh in range(4):
                        pst = psKV[2 * s2 + hh // 2]
                        inc(v.tensor_copy(kv_bf[:, 4 * strip + hh, 0:129],
                                          pst[:, hh % 2, 0:129]), "v_kv")
                if p == 0:
                    ckpt(0.7)
            for hh in range(H):
                inc(v.tensor_copy(kblk[:, hh, hh:hh + 1],
                                  kv_bf[:, hh, 128:129]), "v_kv")
            if "xn" in taps:
                wait(sp, "sp", "v_tr")
                inc(sp.dma_start(out=tap_t["xn"],
                                 in_=B1[:].rearrange("p c t -> p (c t)")),
                    "d_out", 16)
            if "kv" in taps:
                wait(sp, "sp", "v_kv")
                inc(sp.dma_start(out=tap_t["kv"],
                                 in_=kv_bf[:].rearrange("p h e -> p (h e)")),
                    "d_out", 16)

            ckpt(1)
            a_end = cnt("pe")
            esA.close()

            esQ = ExitStack()
            wq_sl = esQ.enter_context(nc.sbuf_tensor("wq_sl", [128, 2, 16, 128], BF16))
            psQ = esQ.enter_context(nc.psum_tensor("psQ", [128, 4, 512], F32))

            wait(g, "g", "pe", a_end)

            wq_used = [None, None]

            def load_wq(fc, b):
                if wq_used[b] is not None:
                    wait(g, "g", "pe", wq_used[b])
                inc(g.dma_start(
                    out=wq_sl[:, b, :, :],
                    in_=wqT_h[:, 128 * fc:128 * fc + 128].rearrange(
                        "(c p) f -> p c f", p=128)), "dc", 16)
                return cnt("dc")

            wq_ready = [load_wq(0, 0), None]
            qp_free = [None] * 4
            first_q_evac = True
            wait(pe, "pe", "v_kv")
            wait(pe, "pe", "v")
            wait(pe, "pe", "a")
            for fc in range(16):
                b = fc % 2
                if fc + 1 < 16:
                    wq_ready[(fc + 1) % 2] = load_wq(fc + 1, (fc + 1) % 2)
                wait(pe, "pe", "dc", wq_ready[b])
                for ts4 in range(4):
                    ps = ts4
                    if qp_free[ps] is not None:
                        wait(pe, "pe", "v", qp_free[ps])
                    for dc in range(16):
                        pe.matmul(psQ[:, ps, :], wq_sl[:, b, dc, :],
                                  B1[:, dc, 512 * ts4:512 * ts4 + 512],
                                  start=(dc == 0), stop=False,
                                  skip_group_check=True)
                    inc(pe.matmul(psQ[:, ps, :],
                                  biasA[0:1, 128 * fc:128 * fc + 128],
                                  ones_bf[0:1, 0:512], start=False, stop=True,
                                  skip_group_check=True), "pe")
                    if ts4 == 3:
                        wq_used[b] = cnt("pe")
                    wait(v, "v", "pe")
                    if first_q_evac:
                        wait(v, "v", "pe", a_end)
                        first_q_evac = False
                    qp_free[ps] = elu_evac(psQ[:, ps, :],
                                           B2[:, fc, 512 * ts4:512 * ts4 + 512])[0]
                drive(gen_r, 4)
            w_all = None
            qe_done = cnt("v")
            q_end = cnt("pe")
            if "qe" in taps:
                wait(sp, "sp", "v", qe_done)
                inc(sp.dma_start(out=tap_t["qe"],
                                 in_=B2[:].rearrange("p c t -> p (c t)")),
                    "d_out", 16)
            ckpt(2)
            esQ.close()

            esZ = ExitStack()
            psZ = esZ.enter_context(nc.psum_tensor("psZ", [16, 512], F32))
            psR = esZ.enter_context(nc.psum_tensor("psR", [128, 2, 512], F32))
            psAt = esZ.enter_context(nc.psum_tensor("psAt", [128, 2, 512], F32))
            rb_sb = esZ.enter_context(nc.sbuf_tensor("rb_sb", [128, 2, 512], F32))

            wait(pe, "pe", "v_kv")
            wait(pe, "pe", "g")
            wait(pe, "pe", "v")
            wait(pe, "pe", "a")
            rr_done = [None] * 4
            for st4 in range(4):
                wait(pe, "pe", "v")
                for c in range(16):
                    mm = pe.matmul(psZ[:], kblk[:, c, :],
                                   B2[:, c, 512 * st4:512 * st4 + 512],
                                   start=(c == 0), stop=(c == 15),
                                   skip_group_check=True)
                inc(mm, "pe")
                wait(v, "v", "pe")
                inc(v.tensor_scalar_add(scr1[0:16, 0:512], psZ[:], EPS_Z), "v")
                wait(a, "a", "v")
                a.activation(out=scr1[0:16, 0:512], in_=scr1[0:16, 0:512],
                             func=AF.Ln)
                rr_done[st4] = inc(a.activation(
                    out=rrow[:, 512 * st4:512 * st4 + 512],
                    in_=scr1[0:16, 0:512], func=AF.Exp, scale=-1.0), "a")
            if "rr" in taps:
                wait(sp, "sp", "a", rr_done[3])
                inc(sp.dma_start(out=tap_t["rr"], in_=rrow[:]), "d_out", 16)

            at_free = [None, None]
            rb_free = [None, None]
            first_at = True
            for st4 in range(4):
                for hh in range(H):
                    sl = hh % 2
                    wait(pe, "pe", "a", rr_done[st4])
                    if rb_free[sl] is not None:
                        wait(pe, "pe", "v", rb_free[sl])
                    pe.matmul(psR[:, sl, :], sel16[:, hh, :],
                              rrow[:, 512 * st4:512 * st4 + 512],
                              start=True, stop=True, skip_group_check=True)
                    if at_free[sl] is not None:
                        wait(pe, "pe", "v", at_free[sl])
                    inc(pe.matmul(psAt[:, sl, :], kv_bf[:, hh, 0:128],
                                  B2[:, hh, 512 * st4:512 * st4 + 512],
                                  start=True, stop=True,
                                  skip_group_check=True), "pe")
                    wait(v, "v", "pe")
                    if first_at:
                        wait(v, "v", "pe", q_end)
                        first_at = False
                    rb_free[sl] = inc(v.tensor_copy(rb_sb[:, sl, :],
                                                    psR[:, sl, :]), "v")
                    at_free[sl] = inc(v.tensor_tensor(
                        B1[:, hh, 512 * st4:512 * st4 + 512],
                        psAt[:, sl, :], rb_sb[:, sl, :], ALU.mult), "v")
            attn_done = cnt("v")
            if "attn" in taps:
                wait(sp, "sp", "v", attn_done)
                inc(sp.dma_start(out=tap_t["attn"],
                                 in_=B1[:].rearrange("p c t -> p (c t)")),
                    "d_out", 16)
            ckpt(3)
            esZ.close()
            esK.close()

            esO = ExitStack()
            psO = esO.enter_context(nc.psum_tensor("psO", [128, 8, 512], F32))
            o_tile = esO.enter_context(nc.sbuf_tensor("o_tile", [128, 2, D], F32))

            wait(g, "g", "pe")
            for hec in range(16):
                inc(g.dma_start(out=B2[:, hec, :],
                                in_=woT_h[128 * hec:128 * hec + 128, :]),
                    "dc", 16)
            wo_ready = cnt("dc")

            o_free = [None, None]
            ot_free = [None, None]
            x1_w = {}
            for tc in range(16):
                ob = tc % 2
                load_x_tile(tc, ob)
                wait(pe, "pe", "dc", wo_ready)
                wait(pe, "pe", "v", attn_done)
                if o_free[ob] is not None:
                    wait(pe, "pe", "a", o_free[ob])
                for gs in range(4):
                    for hec in range(16):
                        pe.matmul(psO[:, 4 * ob + gs, :],
                                  B1[:, hec, 128 * tc:128 * tc + 128],
                                  B2[:, hec, 512 * gs:512 * gs + 512],
                                  start=(hec == 0), stop=False,
                                  skip_group_check=True)
                    inc(pe.matmul(psO[:, 4 * ob + gs, :], ones_bf[0:1, 0:128],
                                  biasB[0:1, 512 * gs:512 * gs + 512],
                                  start=False, stop=True,
                                  skip_group_check=True), "pe")
                wait(a, "a", "pe")
                if ot_free[ob] is not None:
                    wait(a, "a", "d_scr", ot_free[ob])
                o_free[ob] = inc(a.activation(
                    out=o_tile[:, ob, :].rearrange("p (g f) -> p g f", g=4),
                    in_=psO[:, 4 * ob:4 * ob + 4, :], func=AF.Copy), "a")
                wait(v, "v", "a")
                ln_stats(o_tile[:, ob, :])
                v.tensor_scalar(scr1[:, 0:D], o_tile[:, ob, :], cols[:, 4:5],
                                cols[:, 5:6], ALU.mult, ALU.add)
                wait(v, "v", "d_x")
                c1 = inc(v.tensor_add(o_tile[:, ob, :], xt[:, ob, :],
                                      scr1[:, 0:D]), "v")
                x_freed[ob] = ("v", c1)
                wait(sp, "sp", "v", c1)
                x1_w[tc] = inc(sp.dma_start(
                    out=x1_scr[128 * tc:128 * tc + 128, :],
                    in_=o_tile[:, ob, :]), "d_scr", 16)
                ot_free[ob] = x1_w[tc]
                if gen_r is not None:
                    drive(gen_r, 16)
            o_end = cnt("pe")
            if "x1" in taps:
                wait(sp, "sp", "d_scr")
                inc(sp.dma_start(out=tap_t["x1"], in_=x1_scr), "d_out", 16)
            ckpt(4)
            esO.close()

            esF = ExitStack()
            w2out = esF.enter_context(nc.sbuf_tensor("w2out", [128, 4, D], F32))
            fcols = esF.enter_context(nc.sbuf_tensor("fcols", [128, 16], F32))

            HAP = lambda j: B2[:, j // 4, 512 * (j % 4):512 * (j % 4) + 512]
            w2out_free = [None] * 4

            for g4 in range(4):
                esF1 = ExitStack()
                psT2_a = esF1.enter_context(nc.psum_tensor(f"psT2a_{g4}", [128, 4, 128], BF16))
                psT2_b = esF1.enter_context(nc.psum_tensor(f"psT2b_{g4}", [128, 4, 128], BF16))
                psT2 = [psT2_a, psT2_b]
                psF = esF1.enter_context(nc.psum_tensor(f"psF_{g4}", [128, 4, 512], F32))
                psX = esF1.enter_context(nc.psum_tensor(f"psX_{g4}", [1, 4, 128], F32))
                for tc in range(4):
                    gt = 4 * g4 + tc
                    b = tc % 2
                    wait(sp, "sp", "d_scr", x1_w[gt])
                    load_x_tile(gt, b, src=x1_scr)
                    wait(v, "v", "d_x")
                    wait(a, "a", "d_x")
                    if g4 == 0 and tc == 0:
                        wait(v, "v", "pe", o_end)
                    if g4 == 0 and tc == 1 and "xtap" in taps:
                        wait(sp, "sp", "d_x")
                        dn = inc(sp.dma_start(out=tap_t["xtap"],
                                              in_=xt[:, b, :]), "d_out", 16)
                        for e_, en_ in ((v, "v"), (a, "a"), (sp, "sp")):
                            wait(e_, en_, "d_out", dn)
                    ln_stats(xt[:, b, :], want_maxmin=True)
                    if g4 == 0 and tc == 1 and "scr1" in taps:
                        wait(sp, "sp", "v")
                        wait(sp, "sp", "a")
                        dn = inc(sp.dma_start(out=tap_t["scr1"], in_=scr1[:]),
                                 "d_out", 16)
                        for e_, en_ in ((v, "v"), (a, "a")):
                            wait(e_, en_, "d_out", dn)
                    if g4 == 0 and tc == 1 and "colsB" in taps:
                        wait(sp, "sp", "v")
                        wait(sp, "sp", "a")
                        dn = inc(sp.dma_start(out=tap_t["colsB"], in_=cols[:]),
                                 "d_out", 16)
                        for e_, en_ in ((v, "v"), (a, "a")):
                            wait(e_, en_, "d_out", dn)
                    vS(v.tensor_tensor(cols[:, 9:10], cols[:, 6:7], cols[:, 2:3],
                                       ALU.subtract))
                    vS(v.tensor_tensor(cols[:, 10:11], cols[:, 2:3], cols[:, 7:8],
                                       ALU.subtract))
                    vS(v.tensor_tensor(cols[:, 9:10], cols[:, 9:10],
                                       cols[:, 10:11], ALU.max))
                    vS(v.tensor_tensor(cols[:, 9:10], cols[:, 9:10],
                                       cols[:, 4:5], ALU.mult))
                    vS(v.tensor_scalar_max(cols[:, 9:10], cols[:, 9:10], 1e-5))
                    if g4 == 0 and tc == 1 and "colsA" in taps:
                        wait(sp, "sp", "v")
                        dn = inc(sp.dma_start(out=tap_t["colsA"], in_=cols[:]),
                                 "d_out", 16)
                        for e_, en_ in ((v, "v"), (a, "a")):
                            wait(e_, en_, "d_out", dn)
                    wait(v, "v", "v_wsc")
                    vS(v.tensor_tensor(cols[:, 16 + tc:17 + tc], cols[:, 9:10],
                                       wsc[:, 1:2], ALU.mult))
                    vS(v.reciprocal(cols[:, 10:11], cols[:, 9:10]))
                    vS(v.tensor_scalar_mul(cols[:, 10:11], cols[:, 10:11], 127.0))
                    vS(v.tensor_tensor(cols[:, 11:12], cols[:, 4:5],
                                       cols[:, 10:11], ALU.mult))
                    inc(v.tensor_tensor(cols[:, 8:9], cols[:, 5:6],
                                        cols[:, 10:11], ALU.mult), "v")
                    wait(v, "v", "v")
                    c1 = inc(v.tensor_scalar(scr1[:, 0:D], xt[:, b, :],
                                             cols[:, 11:12], cols[:, 8:9],
                                             ALU.mult, ALU.add), "v")
                    x_freed[b] = ("v", c1)
                    for hf in range(2):
                        v.tensor_copy(scri[:], scr1[:, 1024 * hf:1024 * hf + 1024])
                        v.tensor_scalar_mul(B1[:, 12 + 2 * hf, 1536:2048],
                                            scri[:, 0:512],
                                            cols[:, 16 + tc:17 + tc])
                        inc(v.tensor_scalar_mul(B1[:, 13 + 2 * hf, 1536:2048],
                                                scri[:, 512:1024],
                                                cols[:, 16 + tc:17 + tc]), "v")
                    if gen_r is not None:
                        drive(gen_r, 4)
                    wait(pe, "pe", "v", cnt("v"))
                    for q in range(4):
                        pj = q % 2
                        if tr_done[pj] is not None:
                            wait(pe, "pe", "v_tr", tr_done[pj])
                        for m in range(4):
                            inc(pe.transpose(
                                psT2[pj][:, m, :],
                                B1[:, 12 + q, 1536 + 128 * m:1536 + 128 * m + 128],
                                ident_bf[:]), "pe_tr")
                        wait(v, "v", "pe_tr")
                        tr_done[pj] = inc(v.tensor_copy(
                            B1[:, 4 * q:4 * q + 4, 128 * tc:128 * tc + 128],
                            psT2[pj][:]), "v_tr")
                if g4 == 0 and "n1f" in taps:
                    wait(sp, "sp", "v_tr")
                    dn = inc(sp.dma_start(out=tap_t["n1f"],
                             in_=B1[:].rearrange("p c t -> p (c t)")), "d_out", 16)
                    for e_, en_ in ((v, "v"), (a, "a"), (pe, "pe"), (g, "g"),
                                    (sp, "sp")):
                        wait(e_, en_, "d_out", dn)
                if g4 == 0:
                    ckpt(5)
                    assert W1DONE[0] is not None, "W1 quant not driven in time"
                    wait(sp, "sp", "d_wqo", W1DONE[0])

                w1_used = [None, None]

                def load_w1_window(w, b):
                    if w1_used[b] is not None:
                        wait(sp, "sp", "pe", w1_used[b])
                    inc(sp.dma_start(
                        out=B1[:, :, 512 + 512 * b:1024 + 512 * b],
                        in_=w1q_scr[w].rearrange("c p f -> p c f")), "d_w1", 16)
                    return cnt("d_w1")

                w1_ready = [load_w1_window(0, 0), None]
                hs_free = [None] * 4
                first_h = (g4 == 0)
                wait(pe, "pe", "a")
                for w in range(16):
                    b = w % 2
                    if gen_r is not None:
                        drive(gen_r, 8)
                    if w + 1 < 16:
                        w1_ready[(w + 1) % 2] = load_w1_window(w + 1, (w + 1) % 2)
                    wait(pe, "pe", "d_w1", w1_ready[b])
                    wait(pe, "pe", "v_tr")
                    for fb in range(4):
                        j = 4 * w + fb
                        if hs_free[fb] is not None:
                            wait(pe, "pe", "a", hs_free[fb])
                        for dc in range(16):
                            mm = pe.matmul(
                                psF[:, fb, :],
                                B1[:, dc, 512 + 512 * b + 128 * fb:
                                   512 + 512 * b + 128 * fb + 128],
                                B1[:, dc, 0:512],
                                start=(dc == 0), stop=(dc == 15),
                                skip_group_check=True)
                        inc(mm, "pe")
                        if fb == 3:
                            w1_used[b] = cnt("pe")
                        wait(a, "a", "pe")
                        if first_h:
                            wait(a, "a", "pe", o_end)
                            first_h = False
                        hs_free[fb] = inc(a.activation(
                            out=HAP(j), in_=psF[:, fb, :], func=AF.Silu), "a")
                if g4 == 0 and "hs0" in taps:
                    wait(sp, "sp", "a")
                    dn = inc(sp.dma_start(out=tap_t["hs0"],
                             in_=B2[:].rearrange("p c t -> p (c t)")), "d_out", 16)
                    for e_, en_ in ((v, "v"), (a, "a"), (pe, "pe"), (g, "g"),
                                    (sp, "sp")):
                        wait(e_, en_, "d_out", dn)
                if g4 == 0:
                    ckpt(6)
                f2_end_a = cnt("a")
                mxV, mnV = B1[:, 14, 512:1024], B1[:, 14, 1024:1536]
                wait(v, "v", "a", f2_end_a)
                v.tensor_copy(mxV, HAP(0))
                v.tensor_copy(mnV, HAP(0))
                for j in range(1, 64):
                    v.tensor_tensor(mxV, mxV, HAP(j), ALU.max)
                    v.tensor_tensor(mnV, mnV, HAP(j), ALU.min)
                v.tensor_scalar_mul(mnV, mnV, -1.0)
                inc(v.tensor_tensor(mxV, mxV, mnV, ALU.max), "v")
                for tc in range(4):
                    wait(pe, "pe", "v")
                    inc(pe.transpose(psT2[tc % 2][:, 0, :],
                                     B1[:, 14, 512 + 128 * tc:512 + 128 * tc + 128],
                                     ident_bf[:]), "pe_tr")
                    wait(v, "v", "pe_tr")
                    vS(v.tensor_reduce(fcols[:, tc:tc + 1], psT2[tc % 2][:, 0, :],
                                       AX.X, ALU.max))
                vS(v.tensor_scalar_max(fcols[:, 0:4], fcols[:, 0:4], 1e-5))
                for tc in range(4):
                    vS(v.tensor_tensor(fcols[:, 12 + tc:13 + tc],
                                       fcols[:, tc:tc + 1], wsc[:, 3:4], ALU.mult))
                vS(v.reciprocal(fcols[:, 4:8], fcols[:, 0:4]))
                inc(v.tensor_scalar_mul(fcols[:, 4:8], fcols[:, 4:8], 127.0), "v")
                wait(v, "v", "v")
                wait(pe, "pe", "v")
                for tc in range(4):
                    inc(pe.transpose(psX[0:1, tc, :], fcols[:, 4 + tc:5 + tc],
                                     ident_f32[:]), "pe")
                wait(v, "v", "pe")
                inc(v.tensor_copy(scrE3[0:1, 1, :],
                                  psX[0:1, :, :].rearrange("o t c -> o (t c)")),
                    "v")
                wait(pe, "pe", "v")
                wait(pe, "pe", "a")
                inc(pe.matmul(psF[:, 0, :], ones_f32[0:1, :], scrE3[0:1, 1, :],
                              start=True, stop=True, skip_group_check=True), "pe")
                wait(v, "v", "pe")
                inc(v.tensor_copy(scrE3[:, 0, :], psF[:, 0, :]), "v_s2")
                wait(g, "g", "v_s2")
                wait(v, "v", "v_s2")
                for j in range(64):
                    if j % 2 == 0 and gen_r is not None:
                        drive(gen_r, 1)
                    if j % 2 == 0:
                        v.tensor_tensor(scrE1[:], HAP(j), scrE3[:, 0, :], ALU.mult)
                        v.tensor_copy(scri[:, 0:512], scrE1[:])
                        inc(v.tensor_copy(HAP(j), scri[:, 0:512]), "v_n2")
                    else:
                        g.tensor_tensor(scrE2[:], HAP(j), scrE3[:, 0, :], ALU.mult)
                        g.tensor_copy(scri[:, 512:1024], scrE2[:])
                        inc(g.tensor_copy(HAP(j), scri[:, 512:1024]), "g_n2")
                if g4 == 0 and "h0" in taps:
                    wait(sp, "sp", "v_n2")
                    wait(sp, "sp", "g_n2")
                    inc(sp.dma_start(out=tap_t["h0"],
                                     in_=B2[:].rearrange("p c t -> p (c t)")),
                        "d_out", 16)
                if g4 == 0:
                    ckpt(7)
                esF1.close()
                if g4 == 0 and gen_r is not None:
                    drain(gen_r)
                esF2 = ExitStack()
                psW2 = esF2.enter_context(nc.psum_tensor(f"psW2_{g4}", [128, 8, 512], F32))
                wait(pe, "pe", "v_n2")
                wait(pe, "pe", "g_n2")
                wait(sp, "sp", "d_wqo", cnt("d_wqo"))

                def load_w2(gh, fc, b):
                    inc(sp.dma_start(out=B1[:, 8 + 2 * b, 1536:2048],
                                     in_=w2q_scr[fc, :, 1024 * gh:1024 * gh + 512]),
                        "d_w2", 16)
                    inc(sp.dma_start(out=B1[:, 9 + 2 * b, 1536:2048],
                                     in_=w2q_scr[fc, :,
                                                 1024 * gh + 512:1024 * gh + 1024]),
                        "d_w2", 16)
                    return cnt("d_w2")

                w2_used = [None, None]
                w2_ready = [load_w2(0, 0, 0), None]
                gh_evac = None
                for gh in range(2):
                    for fc in range(64):
                        b = fc % 2
                        nb = (fc + 1) % 2
                        if fc + 1 < 64 or gh == 0:
                            if w2_used[nb] is not None:
                                wait(sp, "sp", "pe", w2_used[nb])
                            w2_ready[nb] = (load_w2(gh, fc + 1, nb)
                                            if fc + 1 < 64 else load_w2(1, 0, nb))
                        wait(pe, "pe", "d_w2", w2_ready[b])
                        if fc == 0 and gh == 1 and gh_evac is not None:
                            wait(pe, "pe", "a", gh_evac)
                        for tc in range(4):
                            for gsub in range(2):
                                mm = pe.matmul(
                                    psW2[:, 2 * tc + gsub, :],
                                    B2[:, fc // 4,
                                       512 * (fc % 4) + 128 * tc:
                                       512 * (fc % 4) + 128 * tc + 128],
                                    B1[:, 8 + 2 * b + gsub, 1536:2048],
                                    start=(fc == 0), stop=(fc == 63),
                                    skip_group_check=True)
                        w2_used[b] = inc(mm, "pe")
                    wait(a, "a", "pe")
                    for tc in range(4):
                        if gh == 0 and w2out_free[tc] is not None:
                            wait(a, "a", "d_out", w2out_free[tc])
                        gh_evac = inc(a.activation(
                            out=w2out[:, tc, 1024 * gh:1024 * gh + 1024]
                            .rearrange("p (s f) -> p s f", s=2),
                            in_=psW2[:, 2 * tc:2 * tc + 2, :],
                            func=AF.Copy, scale=fcols[:, 12 + tc:13 + tc]), "a")
                if g4 == 0:
                    ckpt(8)
                esF2.close()
                for tc in range(4):
                    gt = 4 * g4 + tc
                    b = tc % 2
                    load_x_tile(gt, b, src=x1_scr)
                    wait(v, "v", "a", gh_evac)
                    ln_stats(w2out[:, tc, :])
                    v.tensor_scalar(scr1[:, 0:D], w2out[:, tc, :], cols[:, 4:5],
                                    cols[:, 5:6], ALU.mult, ALU.add)
                    wait(v, "v", "d_x")
                    c1 = inc(v.tensor_add(w2out[:, tc, :], xt[:, b, :],
                                          scr1[:, 0:D]), "v")
                    x_freed[b] = ("v", c1)
                    wait(sp, "sp", "v", c1)
                    w2out_free[tc] = inc(sp.dma_start(
                        out=out_h[128 * gt:128 * gt + 128, :],
                        in_=w2out[:, tc, :]), "d_out", 16)
            wait(sp, "sp", "d_out")
            esF.close()
    except StopEmission:
        pass
    return nc


def kernel(L0_attn_hidden_states, L0_attn_q_w, L0_attn_q_b, L0_attn_k_w,
           L0_attn_k_b, L0_attn_v_w, L0_attn_v_b, L0_attn_o_w, L0_attn_o_b,
           L0_ffn_W1, L0_ffn_W2):
    x = np.asarray(L0_attn_hidden_states, dtype=np.float32)
    ct = lambda w: np.ascontiguousarray(np.asarray(w, dtype=np.float32).T)
    wqT, wkT, wvT, woT = (ct(L0_attn_q_w), ct(L0_attn_k_w),
                          ct(L0_attn_v_w), ct(L0_attn_o_w))
    w1T, w2T = ct(L0_ffn_W1), ct(L0_ffn_W2)
    row = lambda b: np.ascontiguousarray(
        np.asarray(b, dtype=np.float32).reshape(1, D))
    bq, bk, bv, bo = (row(L0_attn_q_b), row(L0_attn_k_b),
                      row(L0_attn_v_b), row(L0_attn_o_b))

    nc = build_program()
    in_maps = []
    for c in range(NCORES):
        b, hp = c // 2, c % 2
        xs = x[b] if hp == 0 else np.ascontiguousarray(
            np.concatenate([x[b, T:], x[b, :T]], axis=0))
        in_maps.append({
            "x": np.ascontiguousarray(xs), "wqT": wqT, "wkT": wkT, "wvT": wvT,
            "woT": woT, "w1T": w1T, "w2T": w2T,
            "bq": bq, "bk": bk, "bv": bv, "bo": bo,
        })
    res = run_bass_kernel_spmd(nc, in_maps, core_ids=list(range(NCORES)))
    out = np.empty((4, S, D), dtype=np.float32)
    for c in range(NCORES):
        b, hp = c // 2, c % 2
        out[b, T * hp:T * hp + T, :] = np.asarray(res.results[c]["out"])
    return out


# version 15
# speedup vs baseline: 1.6874x; 1.0079x over previous
import numpy as np
from contextlib import ExitStack

import concourse.bass as bass
import concourse.mybir as mybir
from concourse.bass_utils import run_bass_kernel_spmd

F32 = mybir.dt.float32
BF16 = mybir.dt.bfloat16
I32 = mybir.dt.int32
AF = mybir.ActivationFunctionType
ALU = mybir.AluOpType
AX = mybir.AxisListType

D = 2048
DFF = 8192
T = 2048
S = 4096
H = 16
NT = 16
NTS = 32
EPS_LN = 1e-5
EPS_Z = 1e-6
QCH = 512
NCORES = 8


class StopEmission(Exception):
    pass


def build_program(taps=(), limit=99, no_quant=False):
    taps = set(taps)
    nc = bass.Bass("TRN2", target_bir_lowering=False,
                   detect_race_conditions=False)

    x_h = nc.dram_tensor("x", [S, D], F32, kind="ExternalInput").ap()
    wqT_h = nc.dram_tensor("wqT", [D, D], F32, kind="ExternalInput").ap()
    wkT_h = nc.dram_tensor("wkT", [D, D], F32, kind="ExternalInput").ap()
    wvT_h = nc.dram_tensor("wvT", [D, D], F32, kind="ExternalInput").ap()
    woT_h = nc.dram_tensor("woT", [D, D], F32, kind="ExternalInput").ap()
    w1T_h = nc.dram_tensor("w1T", [D, DFF], F32, kind="ExternalInput").ap()
    w2T_h = nc.dram_tensor("w2T", [DFF, D], F32, kind="ExternalInput").ap()
    bq_h = nc.dram_tensor("bq", [1, D], F32, kind="ExternalInput").ap()
    bk_h = nc.dram_tensor("bk", [1, D], F32, kind="ExternalInput").ap()
    bv_h = nc.dram_tensor("bv", [1, D], F32, kind="ExternalInput").ap()
    bo_h = nc.dram_tensor("bo", [1, D], F32, kind="ExternalInput").ap()
    out_h = nc.dram_tensor("out", [T, D], F32, kind="ExternalOutput").ap()
    x1_scr = nc.dram_tensor("x1_scr", [T, D], F32).ap()
    w1q_scr = nc.dram_tensor("w1q_scr", [16, 16, 128, 512], BF16).ap()
    w2q_scr = nc.dram_tensor("w2q_scr", [64, 128, D], BF16).ap()
    tap_t = {}
    for tname, shape in [("xn", [128, 16 * T]), ("kv", [128, H * 132]),
                         ("qe", [128, 16 * T]), ("attn", [128, 16 * T]),
                         ("h0", [128, 16 * T]), ("rr", [16, T]),
                         ("n1f", [128, 16 * T]), ("hs0", [128, 16 * T])]:
        if tname in taps:
            tap_t[tname] = nc.dram_tensor("tap_" + tname, shape, BF16,
                                          kind="ExternalOutput").ap()
    if "x1" in taps:
        tap_t["x1"] = nc.dram_tensor("tap_x1", [T, D], F32,
                                     kind="ExternalOutput").ap()
    if "wsc" in taps:
        tap_t["wsc"] = nc.dram_tensor("tap_wsc", [128, 8], F32,
                                      kind="ExternalOutput").ap()
    if "colsA" in taps:
        tap_t["colsA"] = nc.dram_tensor("tap_colsA", [128, 64], F32,
                                        kind="ExternalOutput").ap()
    if "colsB" in taps:
        tap_t["colsB"] = nc.dram_tensor("tap_colsB", [128, 64], F32,
                                        kind="ExternalOutput").ap()
    if "scr1" in taps:
        tap_t["scr1"] = nc.dram_tensor("tap_scr1", [128, D], F32,
                                       kind="ExternalOutput").ap()
    if "xtap" in taps:
        tap_t["xtap"] = nc.dram_tensor("tap_xtap", [128, D], F32,
                                       kind="ExternalOutput").ap()

    es = ExitStack()
    try:
      with es:
        sbuf = lambda *aa: es.enter_context(nc.sbuf_tensor(*aa))

        ident_bf = sbuf("ident_bf", [128, 128], BF16)
        ident_f32 = sbuf("ident_f32", [128, 128], F32)
        ones_bf = sbuf("ones_bf", [128, 512], BF16)
        ones_f32 = sbuf("ones_f32", [128, 128], F32)
        ones_col = sbuf("ones_col", [128, 1], F32)
        biasA = sbuf("biasA", [128, D], BF16)
        biasB = sbuf("biasB", [128, D], BF16)
        scr1 = sbuf("scr1", [128, D], F32)
        scrE1 = sbuf("scrE1", [128, 512], F32)
        scrE2 = sbuf("scrE2", [128, 512], F32)
        scrE3 = sbuf("scrE3", [128, 2, 512], F32)
        scri = sbuf("scri", [128, 1024], I32)
        cols = sbuf("cols", [128, 64], F32)
        qrow = sbuf("qrow", [16, 8], F32)
        wsc = sbuf("wsc", [128, 8], F32)
        abscol = sbuf("abscol", [128, 4], F32)
        xt = sbuf("xt", [128, 2, D], F32)
        B1 = sbuf("B1", [128, 16, T], BF16)
        B2 = sbuf("B2", [128, 16, T], BF16)

        C = {"sems": {}, "cnt": {}, "waited": {}}

        def sem(name):
            if name not in C["sems"]:
                C["sems"][name] = es.enter_context(nc.semaphore(name))
                C["cnt"][name] = 0
            return C["sems"][name]

        def inc(instr, name, k=1):
            instr.then_inc(sem(name), k)
            C["cnt"][name] += k
            return C["cnt"][name]

        def cnt(name):
            return C["cnt"].get(name, 0)

        def wait(eng, ename, name, val=None):
            if val is None:
                val = cnt(name)
            if val is None or val <= 0:
                return
            key = (ename, name)
            if C["waited"].get(key, -1) >= val:
                return
            eng.wait_ge(sem(name), val)
            C["waited"][key] = val

        if True:
            sp, g, pe, v, a = nc.sync, nc.gpsimd, nc.tensor, nc.vector, nc.scalar

            def vS(instr):
                c = inc(instr, "v")
                wait(v, "v", "v", c)
                return c

            def aS(instr):
                c = inc(instr, "a")
                wait(a, "a", "a", c)
                return c

            def ckpt(n):
                if limit <= n:
                    inc(sp.dma_start(out=out_h[0:128, :], in_=scr1[:, 0:D]),
                        "d_out", 16)
                    wait(sp, "sp", "d_out")
                    raise StopEmission()

            _CKPT = ckpt
            g.memset(ident_bf[:], 0.0)
            g.affine_select(out=ident_bf[:], in_=ident_bf[:],
                            compare_op=ALU.not_equal, fill=1.0, base=0,
                            pattern=[[-1, 128]], channel_multiplier=1)
            g.memset(ident_f32[:], 0.0)
            g.affine_select(out=ident_f32[:], in_=ident_f32[:],
                            compare_op=ALU.not_equal, fill=1.0, base=0,
                            pattern=[[-1, 128]], channel_multiplier=1)
            ckpt(0.005)
            g.memset(ones_bf[:], 1.0)
            g.memset(ones_f32[:], 1.0)
            g.memset(ones_col[:], 1.0)
            ckpt(0.01)
            g.memset(biasA[:], 0.0)
            inc(g.memset(biasB[:], 0.0), "g")
            ckpt(0.02)
            inc(g.dma_start(out=biasA[0:1, :], in_=bq_h), "dc", 16)
            ckpt(0.03)
            inc(g.dma_start(out=biasA[32:33, :], in_=bk_h), "dc", 16)
            inc(g.dma_start(out=biasA[64:65, :], in_=bv_h), "dc", 16)
            inc(g.dma_start(out=biasB[0:1, :], in_=bo_h), "dc", 16)
            ckpt(0.05)

            def ln_stats(src_ap, want_maxmin=False, width=D):
                v.tensor_reduce(cols[:, 0:1], src_ap, AX.X, ALU.add)
                if want_maxmin:
                    v.tensor_reduce(cols[:, 6:7], src_ap, AX.X, ALU.max)
                    v.tensor_reduce(cols[:, 7:8], src_ap, AX.X, ALU.min)
                wait(a, "a", "v")
                inc(a.activation(out=scr1[:, 0:width], in_=src_ap,
                                 func=AF.Square), "a")
                wait(v, "v", "a")
                vS(v.tensor_reduce(cols[:, 1:2], scr1[:, 0:width], AX.X, ALU.add))
                ninv = 1.0 / float(width)
                vS(v.tensor_scalar_mul(cols[:, 2:3], cols[:, 0:1], ninv))
                vS(v.tensor_scalar_mul(cols[:, 3:4], cols[:, 1:2], ninv))
                vS(v.tensor_tensor(cols[:, 8:9], cols[:, 2:3], cols[:, 2:3],
                                   ALU.mult))
                vS(v.tensor_tensor(cols[:, 3:4], cols[:, 3:4], cols[:, 8:9],
                                   ALU.subtract))
                inc(v.tensor_scalar_add(cols[:, 3:4], cols[:, 3:4], EPS_LN), "v")
                wait(a, "a", "v")
                aS(a.activation(out=cols[:, 3:4], in_=cols[:, 3:4], func=AF.Ln))
                inc(a.activation(out=cols[:, 4:5], in_=cols[:, 3:4],
                                 func=AF.Exp, scale=-0.5), "a")
                wait(v, "v", "a")
                vS(v.tensor_scalar_mul(cols[:, 5:6], cols[:, 2:3], -1.0))
                inc(v.tensor_tensor(cols[:, 5:6], cols[:, 5:6], cols[:, 4:5],
                                    ALU.mult), "v")
                wait(v, "v", "v")

            def elu_evac(ps_ap, dst_ap, par=[0]):
                p = par[0] = 1 - par[0]
                e3 = scrE3[:, p, :]
                c_ps = inc(v.tensor_copy(e3, ps_ap), "v")
                v.tensor_scalar_min(scrE1[:], e3, 0.0)
                inc(v.tensor_scalar_max(scrE2[:], e3, 0.0), "v")
                wait(a, "a", "v")
                inc(a.activation(out=scrE1[:], in_=scrE1[:], func=AF.Exp), "a")
                wait(v, "v", "a")
                c_add = inc(v.tensor_add(dst_ap, scrE1[:], scrE2[:]), "v")
                return c_ps, c_add

            def gen_abssum(wq_in):
                sub = 0
                freed = [None, None]
                for mi, (wh, nch, fw) in enumerate([(w1T_h, 16, DFF),
                                                    (w2T_h, 64, D)]):
                    first = True
                    for ch in range(nch):
                        for sc in range(fw // QCH):
                            b = sub % 2
                            if freed[b] is not None:
                                wait(sp, "sp", "g_abs", freed[b])
                            inc(sp.dma_start(
                                out=wq_in[:, b, :],
                                in_=wh[128 * ch:128 * ch + 128,
                                       QCH * sc:QCH * sc + QCH]), "d_wabs", 16)
                            wait(v, "v", "d_wabs")
                            c0 = inc(v.tensor_reduce(
                                abscol[:, b:b + 1], wq_in[:, b, :], AX.X,
                                ALU.add, apply_absolute_value=True), "g_abs")
                            freed[b] = c0
                            wait(v, "v", "g_abs", c0)
                            if first:
                                inc(v.tensor_copy(cols[:, 32 + mi:33 + mi],
                                                  abscol[:, b:b + 1]), "g_abs")
                                first = False
                            else:
                                inc(v.tensor_tensor(
                                    cols[:, 32 + mi:33 + mi],
                                    cols[:, 32 + mi:33 + mi],
                                    abscol[:, b:b + 1], ALU.add), "g_abs")
                            wait(v, "v", "g_abs")
                            sub += 1
                            yield

            def quant_finalize(psSm):
                wait(pe, "pe", "g_abs")
                inc(pe.matmul(psSm[0:1, 0:2], ones_col[:], cols[:, 32:34],
                              start=True, stop=True, skip_group_check=True), "pe")
                wait(v, "v", "pe")
                vS(v.tensor_scalar(qrow[0:1, 0:2], psSm[0:1, 0:2],
                                   1.0 / float(DFF * D), 1e-5, ALU.mult, ALU.max))
                vS(v.reciprocal(qrow[0:1, 2:4], qrow[0:1, 0:2]))
                inc(v.tensor_scalar_mul(qrow[0:1, 4:6], qrow[0:1, 0:2],
                                        1.0 / 127.0), "v")
                wait(pe, "pe", "v")
                pe.matmul(psSm[:, 0:1], ones_f32[0:1, :], qrow[0:1, 2:3],
                          start=True, stop=True, skip_group_check=True)
                pe.matmul(psSm[:, 1:2], ones_f32[0:1, :], qrow[0:1, 4:5],
                          start=False, stop=True, skip_group_check=True)
                pe.matmul(psSm[:, 2:3], ones_f32[0:1, :], qrow[0:1, 3:4],
                          start=False, stop=True, skip_group_check=True)
                inc(pe.matmul(psSm[:, 3:4], ones_f32[0:1, :], qrow[0:1, 5:6],
                              start=False, stop=True, skip_group_check=True), "pe")
                wait(v, "v", "pe")
                inc(v.tensor_copy(wsc[:, 0:4], psSm[:, 0:4]), "v_wsc")
                if "wsc" in taps:
                    wait(sp, "sp", "v_wsc")
                    inc(sp.dma_start(out=tap_t["wsc"], in_=wsc[:]), "d_out", 16)

            def gen_round(wq_in, wq_bfb, qscr):
                sub = 0
                freed = [None, None]
                out_d = [None, None]
                for (wh, scr_t, nch, fw, sci) in [(w1T_h, w1q_scr, 16, DFF, 0),
                                                  (w2T_h, w2q_scr, 64, D, 2)]:
                    for ch in range(nch):
                        for sc in range(fw // QCH):
                            b = sub % 2
                            if freed[b] is not None:
                                wait(sp, "sp", "v_q", freed[b])
                            inc(sp.dma_start(
                                out=wq_in[:, b, :],
                                in_=wh[128 * ch:128 * ch + 128,
                                       QCH * sc:QCH * sc + QCH]), "d_wrnd", 16)
                            wait(a, "a", "d_wrnd")
                            wait(a, "a", "v_wsc")
                            wait(a, "a", "v_q")
                            inc(a.activation(out=qscr[:], in_=wq_in[:, b, :],
                                             func=AF.Copy,
                                             scale=wsc[:, sci:sci + 1]), "a_q")
                            wait(v, "v", "a_q")
                            v.tensor_scalar_min(qscr[:], qscr[:], 1.0)
                            v.tensor_scalar_max(scri[:, 0:QCH], qscr[:], -1.0)
                            if out_d[b] is not None:
                                wait(v, "v", "d_wqo", out_d[b])
                            freed[b] = inc(v.tensor_copy(wq_bfb[:, b, :],
                                                         scri[:, 0:QCH]), "v_q")
                            wait(sp, "sp", "v_q")
                            if scr_t is w1q_scr:
                                dst = scr_t[sc, ch, :, :]
                            else:
                                dst = scr_t[ch, :, QCH * sc:QCH * sc + QCH]
                            out_d[b] = inc(sp.dma_start(
                                out=dst, in_=wq_bfb[:, b, :]), "d_wqo", 16)
                            sub += 1
                            yield

            def drive(gg, n):
                for _ in range(n):
                    try:
                        next(gg)
                    except StopIteration:
                        return

            def drain(gg):
                for _ in gg:
                    pass

            esK = ExitStack()
            kv_bf = esK.enter_context(nc.sbuf_tensor("kv_bf", [128, H, 132], BF16))
            kblk = esK.enter_context(nc.sbuf_tensor("kblk", [128, H, 16], BF16))
            sel16 = esK.enter_context(nc.sbuf_tensor("sel16", [16, 16, 128], BF16))
            rrow = esK.enter_context(nc.sbuf_tensor("rrow", [16, T], BF16))
            wq_in = esK.enter_context(nc.sbuf_tensor("wq_in", [128, 2, QCH], F32))
            wq_bfb = esK.enter_context(nc.sbuf_tensor("wq_bfb", [128, 2, QCH], BF16))
            qscr = esK.enter_context(nc.sbuf_tensor("qscr", [128, QCH], F32))

            g.memset(kblk[:], 0.0)
            g.memset(sel16[:], 0.0)
            inc(g.affine_select(out=sel16[:], in_=sel16[:],
                                compare_op=ALU.not_equal, fill=1.0, base=0,
                                pattern=[[-1, 16], [0, 128]],
                                channel_multiplier=1), "g")
            ckpt(0.1)

            esA = ExitStack()
            kp_bf = esA.enter_context(nc.sbuf_tensor("kp_bf", [128, 1024], BF16))
            vext = esA.enter_context(nc.sbuf_tensor("vext", [128, 8, 132], BF16))
            xn_tm = esA.enter_context(nc.sbuf_tensor("xn_tm", [128, D], BF16))
            xn_oth = esA.enter_context(nc.sbuf_tensor("xn_oth", [128, 16, 128], BF16))
            psProj = esA.enter_context(nc.psum_tensor("psProj", [128, 2, 512], F32))
            psKV = [esA.enter_context(nc.psum_tensor(f"psKV{i}", [128, 2, 132], F32))
                    for i in range(4)]
            psT_a = esA.enter_context(nc.psum_tensor("psT_a", [128, 4, 128], BF16))
            psT_b = esA.enter_context(nc.psum_tensor("psT_b", [128, 4, 128], BF16))
            psT = [psT_a, psT_b]
            psSm = psProj[:, 0, 0:8]

            inc(g.memset(vext[:], 1.0), "g")

            def load_wkv_pair(p):
                for dc in range(16):
                    inc(g.dma_start(
                        out=B2[:, dc, 0:1024],
                        in_=wkT_h[128 * dc:128 * dc + 128,
                                  1024 * p:1024 * p + 1024]), "dc", 16)
                for dc in range(16):
                    inc(g.dma_start(
                        out=B2[:, dc, 1024:2048],
                        in_=wvT_h[128 * dc:128 * dc + 128,
                                  1024 * p:1024 * p + 1024]), "dc", 16)
                return cnt("dc")

            wkv_ready = load_wkv_pair(0)
            gen_a = gen_abssum(wq_in)
            gen_r = None

            x_freed = [None, None]
            tr_done = [None, None]
            pk_free = None
            pv_free = None
            xn_tm_free = [None]

            def load_x_tile(row, b, src=x_h, tag="d_x"):
                if x_freed[b] is not None:
                    wait(sp, "sp", x_freed[b][0], x_freed[b][1])
                inc(sp.dma_start(out=xt[:, b, :],
                                 in_=src[128 * row:128 * row + 128, :]), tag, 16)

            def build_xn(i, b, dbg=False):
                if dbg:
                    ckpt(0.15)
                wait(v, "v", "d_x")
                wait(a, "a", "d_x")
                ln_stats(xt[:, b, :])
                if dbg:
                    ckpt(0.18)
                if xn_tm_free[0] is not None:
                    wait(v, "v", "pe_tr", xn_tm_free[0])
                c1 = inc(v.tensor_scalar(xn_tm[:], xt[:, b, :], cols[:, 4:5],
                                         cols[:, 5:6], ALU.mult, ALU.add), "v")
                x_freed[b] = ("v", c1)
                if dbg:
                    ckpt(0.19)
                wait(pe, "pe", "v", c1)
                for q in range(4):
                    pj = q % 2
                    if tr_done[pj] is not None:
                        wait(pe, "pe", "v_tr", tr_done[pj])
                    for m in range(4):
                        j = 4 * q + m
                        inc(pe.transpose(psT[pj][:, m, :],
                                         xn_tm[:, 128 * j:128 * j + 128],
                                         ident_bf[:]), "pe_tr")
                    wait(v, "v", "pe_tr")
                    dst = (B1[:, 4 * q:4 * q + 4, 128 * i:128 * i + 128] if i < NT
                           else xn_oth[:, 4 * q:4 * q + 4, :])
                    tr_done[pj] = inc(v.tensor_copy(dst, psT[pj][:]), "v_tr")
                xn_tm_free[0] = cnt("pe_tr")

            for p in range(2):
                if p == 1:
                    if no_quant:
                        ckpt(0.9)
                    quant_finalize(psSm)
                    wait(pe, "pe", "v_wsc")
                    wait(g, "g", "pe")
                    wkv_ready = load_wkv_pair(1)
                    gen_r = gen_round(wq_in, wq_bfb, qscr)
                    wait(pe, "pe", "v_kv")
                for i in range(NTS):
                    b = i % 2
                    own = i < NT
                    if p == 0 or not own:
                        load_x_tile(i, b)
                        build_xn(i, b, dbg=(p == 0 and i == 0))
                    if p == 0 and i == 0:
                        ckpt(0.2)
                    if p == 0 and i == 2:
                        ckpt(0.36)
                    if no_quant:
                        pass
                    elif p == 0:
                        drive(gen_a, 16)
                        if i == NTS - 1:
                            drain(gen_a)
                    else:
                        drive(gen_r, 10)
                    xnsrc = ((lambda dc, i=i: B1[:, dc, 128 * i:128 * i + 128])
                             if own else (lambda dc: xn_oth[:, dc, :]))
                    wait(pe, "pe", "dc", wkv_ready)
                    wait(pe, "pe", "v_tr")
                    for s2 in range(2):
                        strip = 2 * p + s2
                        if pk_free is not None:
                            wait(pe, "pe", "v", pk_free[0])
                            if pk_free[1] is not None:
                                wait(pe, "pe", "a", pk_free[1])
                        for dc in range(16):
                            pe.matmul(psProj[:, 0, :], xnsrc(dc),
                                      B2[:, dc, 512 * s2:512 * s2 + 512],
                                      start=(dc == 0), stop=False,
                                      skip_group_check=True)
                        inc(pe.matmul(psProj[:, 0, :], ones_bf[32:33, 0:128],
                                      biasA[32:33, 512 * strip:512 * strip + 512],
                                      start=False, stop=True,
                                      skip_group_check=True), "pe")
                        wait(v, "v", "pe")
                        cps, kp_add = elu_evac(psProj[:, 0, :],
                                               kp_bf[:, 512 * s2:512 * s2 + 512])
                        pk_free = (cps, None)
                        if pv_free is not None:
                            wait(pe, "pe", "a", pv_free)
                        for dc in range(16):
                            pe.matmul(psProj[:, 1, :], xnsrc(dc),
                                      B2[:, dc, 1024 + 512 * s2:1024 + 512 * s2 + 512],
                                      start=(dc == 0), stop=False,
                                      skip_group_check=True)
                        inc(pe.matmul(psProj[:, 1, :], ones_bf[64:65, 0:128],
                                      biasA[64:65, 512 * strip:512 * strip + 512],
                                      start=False, stop=True,
                                      skip_group_check=True), "pe")
                        wait(a, "a", "pe")
                        pv_free = inc(a.activation(
                            out=vext[:, 4 * s2:4 * s2 + 4, 0:128],
                            in_=psProj[:, 1, :].rearrange("p (h f) -> p h f", h=4),
                            func=AF.Copy), "a")
                        wait(pe, "pe", "v", kp_add)
                        wait(pe, "pe", "a", pv_free)
                        for hh in range(4):
                            pst = psKV[2 * s2 + hh // 2]
                            st = (i == 0 and hh % 2 == 0)
                            mm = pe.matmul(
                                pst[:, hh % 2, 0:129],
                                kp_bf[:, 512 * s2 + 128 * hh:
                                      512 * s2 + 128 * hh + 128],
                                vext[:, 4 * s2 + hh, 0:129],
                                start=st, stop=(i == NTS - 1),
                                skip_group_check=True)
                            if hh == 3:
                                inc(mm, "pe")
                    if p == 0 and i in (0, 1, 2, 3, 15, 17):
                        ckpt({0: 0.3, 1: 0.35, 2: 0.38, 3: 0.4, 15: 0.5, 17: 0.55}[i])
                wait(v, "v", "pe")
                for s2 in range(2):
                    strip = 2 * p + s2
                    for hh in range(4):
                        pst = psKV[2 * s2 + hh // 2]
                        inc(v.tensor_copy(kv_bf[:, 4 * strip + hh, 0:129],
                                          pst[:, hh % 2, 0:129]), "v_kv")
                if p == 0:
                    ckpt(0.7)
            for hh in range(H):
                inc(v.tensor_copy(kblk[:, hh, hh:hh + 1],
                                  kv_bf[:, hh, 128:129]), "v_kv")
            if "xn" in taps:
                wait(sp, "sp", "v_tr")
                inc(sp.dma_start(out=tap_t["xn"],
                                 in_=B1[:].rearrange("p c t -> p (c t)")),
                    "d_out", 16)
            if "kv" in taps:
                wait(sp, "sp", "v_kv")
                inc(sp.dma_start(out=tap_t["kv"],
                                 in_=kv_bf[:].rearrange("p h e -> p (h e)")),
                    "d_out", 16)

            ckpt(1)
            a_end = cnt("pe")
            esA.close()

            esQ = ExitStack()
            wq_sl = esQ.enter_context(nc.sbuf_tensor("wq_sl", [128, 2, 16, 128], BF16))
            psQ = esQ.enter_context(nc.psum_tensor("psQ", [128, 4, 512], F32))

            wait(g, "g", "pe", a_end)

            wq_used = [None, None]

            def load_wq(fc, b):
                if wq_used[b] is not None:
                    wait(g, "g", "pe", wq_used[b])
                inc(g.dma_start(
                    out=wq_sl[:, b, :, :],
                    in_=wqT_h[:, 128 * fc:128 * fc + 128].rearrange(
                        "(c p) f -> p c f", p=128)), "dc", 16)
                return cnt("dc")

            wq_ready = [load_wq(0, 0), None]
            qp_free = [None] * 4
            first_q_evac = True
            wait(pe, "pe", "v_kv")
            wait(pe, "pe", "v")
            wait(pe, "pe", "a")
            for fc in range(16):
                b = fc % 2
                if fc + 1 < 16:
                    wq_ready[(fc + 1) % 2] = load_wq(fc + 1, (fc + 1) % 2)
                wait(pe, "pe", "dc", wq_ready[b])
                for ts4 in range(4):
                    ps = ts4
                    if qp_free[ps] is not None:
                        wait(pe, "pe", "v", qp_free[ps])
                    for dc in range(16):
                        pe.matmul(psQ[:, ps, :], wq_sl[:, b, dc, :],
                                  B1[:, dc, 512 * ts4:512 * ts4 + 512],
                                  start=(dc == 0), stop=False,
                                  skip_group_check=True)
                    inc(pe.matmul(psQ[:, ps, :],
                                  biasA[0:1, 128 * fc:128 * fc + 128],
                                  ones_bf[0:1, 0:512], start=False, stop=True,
                                  skip_group_check=True), "pe")
                    if ts4 == 3:
                        wq_used[b] = cnt("pe")
                    wait(v, "v", "pe")
                    if first_q_evac:
                        wait(v, "v", "pe", a_end)
                        first_q_evac = False
                    qp_free[ps] = elu_evac(psQ[:, ps, :],
                                           B2[:, fc, 512 * ts4:512 * ts4 + 512])[0]
                drive(gen_r, 12)
            drain(gen_r)
            w_all = cnt("d_wqo")
            qe_done = cnt("v")
            q_end = cnt("pe")
            if "qe" in taps:
                wait(sp, "sp", "v", qe_done)
                inc(sp.dma_start(out=tap_t["qe"],
                                 in_=B2[:].rearrange("p c t -> p (c t)")),
                    "d_out", 16)
            ckpt(2)
            esQ.close()

            esZ = ExitStack()
            psZ = esZ.enter_context(nc.psum_tensor("psZ", [16, 512], F32))
            psR = esZ.enter_context(nc.psum_tensor("psR", [128, 2, 512], F32))
            psAt = esZ.enter_context(nc.psum_tensor("psAt", [128, 2, 512], F32))
            rb_sb = esZ.enter_context(nc.sbuf_tensor("rb_sb", [128, 2, 512], F32))

            wait(pe, "pe", "v_kv")
            wait(pe, "pe", "g")
            wait(pe, "pe", "v")
            wait(pe, "pe", "a")
            rr_done = [None] * 4
            for st4 in range(4):
                wait(pe, "pe", "v")
                for c in range(16):
                    mm = pe.matmul(psZ[:], kblk[:, c, :],
                                   B2[:, c, 512 * st4:512 * st4 + 512],
                                   start=(c == 0), stop=(c == 15),
                                   skip_group_check=True)
                inc(mm, "pe")
                wait(v, "v", "pe")
                inc(v.tensor_scalar_add(scr1[0:16, 0:512], psZ[:], EPS_Z), "v")
                wait(a, "a", "v")
                a.activation(out=scr1[0:16, 0:512], in_=scr1[0:16, 0:512],
                             func=AF.Ln)
                rr_done[st4] = inc(a.activation(
                    out=rrow[:, 512 * st4:512 * st4 + 512],
                    in_=scr1[0:16, 0:512], func=AF.Exp, scale=-1.0), "a")
            if "rr" in taps:
                wait(sp, "sp", "a", rr_done[3])
                inc(sp.dma_start(out=tap_t["rr"], in_=rrow[:]), "d_out", 16)

            at_free = [None, None]
            rb_free = [None, None]
            first_at = True
            for st4 in range(4):
                for hh in range(H):
                    sl = hh % 2
                    wait(pe, "pe", "a", rr_done[st4])
                    if rb_free[sl] is not None:
                        wait(pe, "pe", "v", rb_free[sl])
                    pe.matmul(psR[:, sl, :], sel16[:, hh, :],
                              rrow[:, 512 * st4:512 * st4 + 512],
                              start=True, stop=True, skip_group_check=True)
                    if at_free[sl] is not None:
                        wait(pe, "pe", "v", at_free[sl])
                    inc(pe.matmul(psAt[:, sl, :], kv_bf[:, hh, 0:128],
                                  B2[:, hh, 512 * st4:512 * st4 + 512],
                                  start=True, stop=True,
                                  skip_group_check=True), "pe")
                    wait(v, "v", "pe")
                    if first_at:
                        wait(v, "v", "pe", q_end)
                        first_at = False
                    rb_free[sl] = inc(v.tensor_copy(rb_sb[:, sl, :],
                                                    psR[:, sl, :]), "v")
                    at_free[sl] = inc(v.tensor_tensor(
                        B1[:, hh, 512 * st4:512 * st4 + 512],
                        psAt[:, sl, :], rb_sb[:, sl, :], ALU.mult), "v")
            attn_done = cnt("v")
            if "attn" in taps:
                wait(sp, "sp", "v", attn_done)
                inc(sp.dma_start(out=tap_t["attn"],
                                 in_=B1[:].rearrange("p c t -> p (c t)")),
                    "d_out", 16)
            ckpt(3)
            esZ.close()
            esK.close()

            esO = ExitStack()
            psO = esO.enter_context(nc.psum_tensor("psO", [128, 8, 512], F32))
            o_tile = esO.enter_context(nc.sbuf_tensor("o_tile", [128, 2, D], F32))

            wait(g, "g", "pe")
            for hec in range(16):
                inc(g.dma_start(out=B2[:, hec, :],
                                in_=woT_h[128 * hec:128 * hec + 128, :]),
                    "dc", 16)
            wo_ready = cnt("dc")

            o_free = [None, None]
            ot_free = [None, None]
            x1_w = {}
            for tc in range(16):
                ob = tc % 2
                load_x_tile(tc, ob)
                wait(pe, "pe", "dc", wo_ready)
                wait(pe, "pe", "v", attn_done)
                if o_free[ob] is not None:
                    wait(pe, "pe", "a", o_free[ob])
                for gs in range(4):
                    for hec in range(16):
                        pe.matmul(psO[:, 4 * ob + gs, :],
                                  B1[:, hec, 128 * tc:128 * tc + 128],
                                  B2[:, hec, 512 * gs:512 * gs + 512],
                                  start=(hec == 0), stop=False,
                                  skip_group_check=True)
                    inc(pe.matmul(psO[:, 4 * ob + gs, :], ones_bf[0:1, 0:128],
                                  biasB[0:1, 512 * gs:512 * gs + 512],
                                  start=False, stop=True,
                                  skip_group_check=True), "pe")
                wait(a, "a", "pe")
                if ot_free[ob] is not None:
                    wait(a, "a", "d_scr", ot_free[ob])
                o_free[ob] = inc(a.activation(
                    out=o_tile[:, ob, :].rearrange("p (g f) -> p g f", g=4),
                    in_=psO[:, 4 * ob:4 * ob + 4, :], func=AF.Copy), "a")
                wait(v, "v", "a")
                ln_stats(o_tile[:, ob, :])
                v.tensor_scalar(scr1[:, 0:D], o_tile[:, ob, :], cols[:, 4:5],
                                cols[:, 5:6], ALU.mult, ALU.add)
                wait(v, "v", "d_x")
                c1 = inc(v.tensor_add(o_tile[:, ob, :], xt[:, ob, :],
                                      scr1[:, 0:D]), "v")
                x_freed[ob] = ("v", c1)
                wait(sp, "sp", "v", c1)
                x1_w[tc] = inc(sp.dma_start(
                    out=x1_scr[128 * tc:128 * tc + 128, :],
                    in_=o_tile[:, ob, :]), "d_scr", 16)
                ot_free[ob] = x1_w[tc]
            o_end = cnt("pe")
            if "x1" in taps:
                wait(sp, "sp", "d_scr")
                inc(sp.dma_start(out=tap_t["x1"], in_=x1_scr), "d_out", 16)
            ckpt(4)
            esO.close()

            esF = ExitStack()
            w2out = esF.enter_context(nc.sbuf_tensor("w2out", [128, 4, D], F32))
            fcols = esF.enter_context(nc.sbuf_tensor("fcols", [128, 16], F32))

            HAP = lambda j: B2[:, j // 4, 512 * (j % 4):512 * (j % 4) + 512]
            w2out_free = [None] * 4

            for g4 in range(4):
                esF1 = ExitStack()
                psT2_a = esF1.enter_context(nc.psum_tensor(f"psT2a_{g4}", [128, 4, 128], BF16))
                psT2_b = esF1.enter_context(nc.psum_tensor(f"psT2b_{g4}", [128, 4, 128], BF16))
                psT2 = [psT2_a, psT2_b]
                psF = esF1.enter_context(nc.psum_tensor(f"psF_{g4}", [128, 4, 512], F32))
                psX = esF1.enter_context(nc.psum_tensor(f"psX_{g4}", [1, 4, 128], F32))
                for tc in range(4):
                    gt = 4 * g4 + tc
                    b = tc % 2
                    wait(sp, "sp", "d_scr", x1_w[gt])
                    load_x_tile(gt, b, src=x1_scr)
                    wait(v, "v", "d_x")
                    wait(a, "a", "d_x")
                    if g4 == 0 and tc == 0:
                        wait(v, "v", "pe", o_end)
                    if g4 == 0 and tc == 1 and "xtap" in taps:
                        wait(sp, "sp", "d_x")
                        dn = inc(sp.dma_start(out=tap_t["xtap"],
                                              in_=xt[:, b, :]), "d_out", 16)
                        for e_, en_ in ((v, "v"), (a, "a"), (sp, "sp")):
                            wait(e_, en_, "d_out", dn)
                    ln_stats(xt[:, b, :], want_maxmin=True)
                    if g4 == 0 and tc == 1 and "scr1" in taps:
                        wait(sp, "sp", "v")
                        wait(sp, "sp", "a")
                        dn = inc(sp.dma_start(out=tap_t["scr1"], in_=scr1[:]),
                                 "d_out", 16)
                        for e_, en_ in ((v, "v"), (a, "a")):
                            wait(e_, en_, "d_out", dn)
                    if g4 == 0 and tc == 1 and "colsB" in taps:
                        wait(sp, "sp", "v")
                        wait(sp, "sp", "a")
                        dn = inc(sp.dma_start(out=tap_t["colsB"], in_=cols[:]),
                                 "d_out", 16)
                        for e_, en_ in ((v, "v"), (a, "a")):
                            wait(e_, en_, "d_out", dn)
                    vS(v.tensor_tensor(cols[:, 9:10], cols[:, 6:7], cols[:, 2:3],
                                       ALU.subtract))
                    vS(v.tensor_tensor(cols[:, 10:11], cols[:, 2:3], cols[:, 7:8],
                                       ALU.subtract))
                    vS(v.tensor_tensor(cols[:, 9:10], cols[:, 9:10],
                                       cols[:, 10:11], ALU.max))
                    vS(v.tensor_tensor(cols[:, 9:10], cols[:, 9:10],
                                       cols[:, 4:5], ALU.mult))
                    vS(v.tensor_scalar_max(cols[:, 9:10], cols[:, 9:10], 1e-5))
                    if g4 == 0 and tc == 1 and "colsA" in taps:
                        wait(sp, "sp", "v")
                        dn = inc(sp.dma_start(out=tap_t["colsA"], in_=cols[:]),
                                 "d_out", 16)
                        for e_, en_ in ((v, "v"), (a, "a")):
                            wait(e_, en_, "d_out", dn)
                    wait(v, "v", "v_wsc")
                    vS(v.tensor_tensor(cols[:, 16 + tc:17 + tc], cols[:, 9:10],
                                       wsc[:, 1:2], ALU.mult))
                    vS(v.reciprocal(cols[:, 10:11], cols[:, 9:10]))
                    vS(v.tensor_scalar_mul(cols[:, 10:11], cols[:, 10:11], 127.0))
                    vS(v.tensor_tensor(cols[:, 11:12], cols[:, 4:5],
                                       cols[:, 10:11], ALU.mult))
                    inc(v.tensor_tensor(cols[:, 8:9], cols[:, 5:6],
                                        cols[:, 10:11], ALU.mult), "v")
                    wait(v, "v", "v")
                    c1 = inc(v.tensor_scalar(scr1[:, 0:D], xt[:, b, :],
                                             cols[:, 11:12], cols[:, 8:9],
                                             ALU.mult, ALU.add), "v")
                    x_freed[b] = ("v", c1)
                    for hf in range(2):
                        v.tensor_copy(scri[:], scr1[:, 1024 * hf:1024 * hf + 1024])
                        v.tensor_scalar_mul(B1[:, 12 + 2 * hf, 1536:2048],
                                            scri[:, 0:512],
                                            cols[:, 16 + tc:17 + tc])
                        inc(v.tensor_scalar_mul(B1[:, 13 + 2 * hf, 1536:2048],
                                                scri[:, 512:1024],
                                                cols[:, 16 + tc:17 + tc]), "v")
                    wait(pe, "pe", "v", cnt("v"))
                    for q in range(4):
                        pj = q % 2
                        if tr_done[pj] is not None:
                            wait(pe, "pe", "v_tr", tr_done[pj])
                        for m in range(4):
                            inc(pe.transpose(
                                psT2[pj][:, m, :],
                                B1[:, 12 + q, 1536 + 128 * m:1536 + 128 * m + 128],
                                ident_bf[:]), "pe_tr")
                        wait(v, "v", "pe_tr")
                        tr_done[pj] = inc(v.tensor_copy(
                            B1[:, 4 * q:4 * q + 4, 128 * tc:128 * tc + 128],
                            psT2[pj][:]), "v_tr")
                if g4 == 0 and "n1f" in taps:
                    wait(sp, "sp", "v_tr")
                    dn = inc(sp.dma_start(out=tap_t["n1f"],
                             in_=B1[:].rearrange("p c t -> p (c t)")), "d_out", 16)
                    for e_, en_ in ((v, "v"), (a, "a"), (pe, "pe"), (g, "g"),
                                    (sp, "sp")):
                        wait(e_, en_, "d_out", dn)
                if g4 == 0:
                    ckpt(5)
                    wait(sp, "sp", "d_wqo", w_all)

                w1_used = [None, None]

                def load_w1_window(w, b):
                    if w1_used[b] is not None:
                        wait(sp, "sp", "pe", w1_used[b])
                    inc(sp.dma_start(
                        out=B1[:, :, 512 + 512 * b:1024 + 512 * b],
                        in_=w1q_scr[w].rearrange("c p f -> p c f")), "d_w1", 16)
                    return cnt("d_w1")

                w1_ready = [load_w1_window(0, 0), None]
                hs_free = [None] * 4
                first_h = (g4 == 0)
                wait(pe, "pe", "a")
                for w in range(16):
                    b = w % 2
                    if w + 1 < 16:
                        w1_ready[(w + 1) % 2] = load_w1_window(w + 1, (w + 1) % 2)
                    wait(pe, "pe", "d_w1", w1_ready[b])
                    wait(pe, "pe", "v_tr")
                    for fb in range(4):
                        j = 4 * w + fb
                        if hs_free[fb] is not None:
                            wait(pe, "pe", "a", hs_free[fb])
                        for dc in range(16):
                            mm = pe.matmul(
                                psF[:, fb, :],
                                B1[:, dc, 512 + 512 * b + 128 * fb:
                                   512 + 512 * b + 128 * fb + 128],
                                B1[:, dc, 0:512],
                                start=(dc == 0), stop=(dc == 15),
                                skip_group_check=True)
                        inc(mm, "pe")
                        if fb == 3:
                            w1_used[b] = cnt("pe")
                        wait(a, "a", "pe")
                        if first_h:
                            wait(a, "a", "pe", o_end)
                            first_h = False
                        hs_free[fb] = inc(a.activation(
                            out=HAP(j), in_=psF[:, fb, :], func=AF.Silu), "a")
                if g4 == 0 and "hs0" in taps:
                    wait(sp, "sp", "a")
                    dn = inc(sp.dma_start(out=tap_t["hs0"],
                             in_=B2[:].rearrange("p c t -> p (c t)")), "d_out", 16)
                    for e_, en_ in ((v, "v"), (a, "a"), (pe, "pe"), (g, "g"),
                                    (sp, "sp")):
                        wait(e_, en_, "d_out", dn)
                if g4 == 0:
                    ckpt(6)
                f2_end_a = cnt("a")
                mxV, mnV = B1[:, 14, 512:1024], B1[:, 14, 1024:1536]
                wait(v, "v", "a", f2_end_a)
                v.tensor_copy(mxV, HAP(0))
                v.tensor_copy(mnV, HAP(0))
                for j in range(1, 64):
                    v.tensor_tensor(mxV, mxV, HAP(j), ALU.max)
                    v.tensor_tensor(mnV, mnV, HAP(j), ALU.min)
                v.tensor_scalar_mul(mnV, mnV, -1.0)
                inc(v.tensor_tensor(mxV, mxV, mnV, ALU.max), "v")
                for tc in range(4):
                    wait(pe, "pe", "v")
                    inc(pe.transpose(psT2[tc % 2][:, 0, :],
                                     B1[:, 14, 512 + 128 * tc:512 + 128 * tc + 128],
                                     ident_bf[:]), "pe_tr")
                    wait(v, "v", "pe_tr")
                    vS(v.tensor_reduce(fcols[:, tc:tc + 1], psT2[tc % 2][:, 0, :],
                                       AX.X, ALU.max))
                vS(v.tensor_scalar_max(fcols[:, 0:4], fcols[:, 0:4], 1e-5))
                for tc in range(4):
                    vS(v.tensor_tensor(fcols[:, 12 + tc:13 + tc],
                                       fcols[:, tc:tc + 1], wsc[:, 3:4], ALU.mult))
                vS(v.reciprocal(fcols[:, 4:8], fcols[:, 0:4]))
                inc(v.tensor_scalar_mul(fcols[:, 4:8], fcols[:, 4:8], 127.0), "v")
                wait(v, "v", "v")
                wait(pe, "pe", "v")
                for tc in range(4):
                    inc(pe.transpose(psX[0:1, tc, :], fcols[:, 4 + tc:5 + tc],
                                     ident_f32[:]), "pe")
                wait(v, "v", "pe")
                inc(v.tensor_copy(scrE3[0:1, 1, :],
                                  psX[0:1, :, :].rearrange("o t c -> o (t c)")),
                    "v")
                wait(pe, "pe", "v")
                wait(pe, "pe", "a")
                inc(pe.matmul(psF[:, 0, :], ones_f32[0:1, :], scrE3[0:1, 1, :],
                              start=True, stop=True, skip_group_check=True), "pe")
                wait(v, "v", "pe")
                inc(v.tensor_copy(scrE3[:, 0, :], psF[:, 0, :]), "v_s2")
                wait(g, "g", "v_s2")
                wait(v, "v", "v_s2")
                for j in range(64):
                    if j % 2 == 0:
                        v.tensor_tensor(scrE1[:], HAP(j), scrE3[:, 0, :], ALU.mult)
                        v.tensor_copy(scri[:, 0:512], scrE1[:])
                        inc(v.tensor_copy(HAP(j), scri[:, 0:512]), "v_n2")
                    else:
                        g.tensor_tensor(scrE2[:], HAP(j), scrE3[:, 0, :], ALU.mult)
                        g.tensor_copy(scri[:, 512:1024], scrE2[:])
                        inc(g.tensor_copy(HAP(j), scri[:, 512:1024]), "g_n2")
                if g4 == 0 and "h0" in taps:
                    wait(sp, "sp", "v_n2")
                    wait(sp, "sp", "g_n2")
                    inc(sp.dma_start(out=tap_t["h0"],
                                     in_=B2[:].rearrange("p c t -> p (c t)")),
                        "d_out", 16)
                if g4 == 0:
                    ckpt(7)
                esF1.close()
                esF2 = ExitStack()
                psW2 = esF2.enter_context(nc.psum_tensor(f"psW2_{g4}", [128, 8, 512], F32))
                wait(pe, "pe", "v_n2")
                wait(pe, "pe", "g_n2")

                def load_w2(gh, fc, b):
                    inc(sp.dma_start(out=B1[:, 8 + 2 * b, 1536:2048],
                                     in_=w2q_scr[fc, :, 1024 * gh:1024 * gh + 512]),
                        "d_w2", 16)
                    inc(sp.dma_start(out=B1[:, 9 + 2 * b, 1536:2048],
                                     in_=w2q_scr[fc, :,
                                                 1024 * gh + 512:1024 * gh + 1024]),
                        "d_w2", 16)
                    return cnt("d_w2")

                w2_used = [None, None]
                w2_ready = [load_w2(0, 0, 0), None]
                gh_evac = None
                for gh in range(2):
                    for fc in range(64):
                        b = fc % 2
                        nb = (fc + 1) % 2
                        if fc + 1 < 64 or gh == 0:
                            if w2_used[nb] is not None:
                                wait(sp, "sp", "pe", w2_used[nb])
                            w2_ready[nb] = (load_w2(gh, fc + 1, nb)
                                            if fc + 1 < 64 else load_w2(1, 0, nb))
                        wait(pe, "pe", "d_w2", w2_ready[b])
                        if fc == 0 and gh == 1 and gh_evac is not None:
                            wait(pe, "pe", "a", gh_evac)
                        for tc in range(4):
                            for gsub in range(2):
                                mm = pe.matmul(
                                    psW2[:, 2 * tc + gsub, :],
                                    B2[:, fc // 4,
                                       512 * (fc % 4) + 128 * tc:
                                       512 * (fc % 4) + 128 * tc + 128],
                                    B1[:, 8 + 2 * b + gsub, 1536:2048],
                                    start=(fc == 0), stop=(fc == 63),
                                    skip_group_check=True)
                        w2_used[b] = inc(mm, "pe")
                    wait(a, "a", "pe")
                    for tc in range(4):
                        if gh == 0 and w2out_free[tc] is not None:
                            wait(a, "a", "d_out", w2out_free[tc])
                        gh_evac = inc(a.activation(
                            out=w2out[:, tc, 1024 * gh:1024 * gh + 1024]
                            .rearrange("p (s f) -> p s f", s=2),
                            in_=psW2[:, 2 * tc:2 * tc + 2, :],
                            func=AF.Copy, scale=fcols[:, 12 + tc:13 + tc]), "a")
                if g4 == 0:
                    ckpt(8)
                esF2.close()
                for tc in range(4):
                    gt = 4 * g4 + tc
                    b = tc % 2
                    load_x_tile(gt, b, src=x1_scr)
                    wait(v, "v", "a", gh_evac)
                    ln_stats(w2out[:, tc, :])
                    v.tensor_scalar(scr1[:, 0:D], w2out[:, tc, :], cols[:, 4:5],
                                    cols[:, 5:6], ALU.mult, ALU.add)
                    wait(v, "v", "d_x")
                    c1 = inc(v.tensor_add(w2out[:, tc, :], xt[:, b, :],
                                          scr1[:, 0:D]), "v")
                    x_freed[b] = ("v", c1)
                    wait(sp, "sp", "v", c1)
                    w2out_free[tc] = inc(sp.dma_start(
                        out=out_h[128 * gt:128 * gt + 128, :],
                        in_=w2out[:, tc, :]), "d_out", 16)
            wait(sp, "sp", "d_out")
            esF.close()
    except StopEmission:
        pass
    return nc


def kernel(L0_attn_hidden_states, L0_attn_q_w, L0_attn_q_b, L0_attn_k_w,
           L0_attn_k_b, L0_attn_v_w, L0_attn_v_b, L0_attn_o_w, L0_attn_o_b,
           L0_ffn_W1, L0_ffn_W2):
    x = np.asarray(L0_attn_hidden_states, dtype=np.float32)
    ct = lambda w: np.ascontiguousarray(np.asarray(w, dtype=np.float32).T)
    wqT, wkT, wvT, woT = (ct(L0_attn_q_w), ct(L0_attn_k_w),
                          ct(L0_attn_v_w), ct(L0_attn_o_w))
    w1T, w2T = ct(L0_ffn_W1), ct(L0_ffn_W2)
    row = lambda b: np.ascontiguousarray(
        np.asarray(b, dtype=np.float32).reshape(1, D))
    bq, bk, bv, bo = (row(L0_attn_q_b), row(L0_attn_k_b),
                      row(L0_attn_v_b), row(L0_attn_o_b))

    nc = build_program()
    in_maps = []
    for c in range(NCORES):
        b, hp = c // 2, c % 2
        xs = x[b] if hp == 0 else np.ascontiguousarray(
            np.concatenate([x[b, T:], x[b, :T]], axis=0))
        in_maps.append({
            "x": np.ascontiguousarray(xs), "wqT": wqT, "wkT": wkT, "wvT": wvT,
            "woT": woT, "w1T": w1T, "w2T": w2T,
            "bq": bq, "bk": bk, "bv": bv, "bo": bo,
        })
    res = run_bass_kernel_spmd(nc, in_maps, core_ids=list(range(NCORES)))
    out = np.empty((4, S, D), dtype=np.float32)
    for c in range(NCORES):
        b, hp = c // 2, c % 2
        out[b, T * hp:T * hp + T, :] = np.asarray(res.results[c]["out"])
    return out


# version 16 (final)
# speedup vs baseline: 1.7227x; 1.0209x over previous
import numpy as np
from contextlib import ExitStack

import concourse.bass as bass
import concourse.mybir as mybir
from concourse.bass_utils import run_bass_kernel_spmd

F32 = mybir.dt.float32
BF16 = mybir.dt.bfloat16
I32 = mybir.dt.int32
AF = mybir.ActivationFunctionType
ALU = mybir.AluOpType
AX = mybir.AxisListType

D = 2048
DFF = 8192
T = 2048
S = 4096
H = 16
NT = 16
NTS = 32
EPS_LN = 1e-5
EPS_Z = 1e-6
QCH = 512
NCORES = 8


class StopEmission(Exception):
    pass


def build_program(taps=(), limit=99, no_quant=False):
    taps = set(taps)
    nc = bass.Bass("TRN2", target_bir_lowering=False,
                   detect_race_conditions=False)

    x_h = nc.dram_tensor("x", [S, D], F32, kind="ExternalInput").ap()
    wqT_h = nc.dram_tensor("wqT", [D, D], F32, kind="ExternalInput").ap()
    wkT_h = nc.dram_tensor("wkT", [D, D], F32, kind="ExternalInput").ap()
    wvT_h = nc.dram_tensor("wvT", [D, D], F32, kind="ExternalInput").ap()
    woT_h = nc.dram_tensor("woT", [D, D], F32, kind="ExternalInput").ap()
    w1T_h = nc.dram_tensor("w1T", [D, DFF], F32, kind="ExternalInput").ap()
    w2T_h = nc.dram_tensor("w2T", [DFF, D], F32, kind="ExternalInput").ap()
    bq_h = nc.dram_tensor("bq", [1, D], F32, kind="ExternalInput").ap()
    bk_h = nc.dram_tensor("bk", [1, D], F32, kind="ExternalInput").ap()
    bv_h = nc.dram_tensor("bv", [1, D], F32, kind="ExternalInput").ap()
    bo_h = nc.dram_tensor("bo", [1, D], F32, kind="ExternalInput").ap()
    out_h = nc.dram_tensor("out", [T, D], F32, kind="ExternalOutput").ap()
    x1_scr = nc.dram_tensor("x1_scr", [T, D], F32).ap()
    w1q_scr = nc.dram_tensor("w1q_scr", [16, 16, 128, 512], BF16).ap()
    w2q_scr = nc.dram_tensor("w2q_scr", [64, 128, D], BF16).ap()
    tap_t = {}
    for tname, shape in [("xn", [128, 16 * T]), ("kv", [128, H * 132]),
                         ("qe", [128, 16 * T]), ("attn", [128, 16 * T]),
                         ("h0", [128, 16 * T]), ("rr", [16, T]),
                         ("n1f", [128, 16 * T]), ("hs0", [128, 16 * T])]:
        if tname in taps:
            tap_t[tname] = nc.dram_tensor("tap_" + tname, shape, BF16,
                                          kind="ExternalOutput").ap()
    if "x1" in taps:
        tap_t["x1"] = nc.dram_tensor("tap_x1", [T, D], F32,
                                     kind="ExternalOutput").ap()
    if "wsc" in taps:
        tap_t["wsc"] = nc.dram_tensor("tap_wsc", [128, 8], F32,
                                      kind="ExternalOutput").ap()
    if "colsA" in taps:
        tap_t["colsA"] = nc.dram_tensor("tap_colsA", [128, 64], F32,
                                        kind="ExternalOutput").ap()
    if "colsB" in taps:
        tap_t["colsB"] = nc.dram_tensor("tap_colsB", [128, 64], F32,
                                        kind="ExternalOutput").ap()
    if "scr1" in taps:
        tap_t["scr1"] = nc.dram_tensor("tap_scr1", [128, D], F32,
                                       kind="ExternalOutput").ap()
    if "xtap" in taps:
        tap_t["xtap"] = nc.dram_tensor("tap_xtap", [128, D], F32,
                                       kind="ExternalOutput").ap()

    es = ExitStack()
    try:
      with es:
        sbuf = lambda *aa: es.enter_context(nc.sbuf_tensor(*aa))

        ident_bf = sbuf("ident_bf", [128, 128], BF16)
        ident_f32 = sbuf("ident_f32", [128, 128], F32)
        ones_bf = sbuf("ones_bf", [128, 512], BF16)
        ones_f32 = sbuf("ones_f32", [128, 128], F32)
        ones_col = sbuf("ones_col", [128, 1], F32)
        biasA = sbuf("biasA", [128, D], BF16)
        biasB = sbuf("biasB", [128, D], BF16)
        scr1 = sbuf("scr1", [128, D], F32)
        scrE1 = sbuf("scrE1", [128, 512], F32)
        scrE2 = sbuf("scrE2", [128, 512], F32)
        scrE3 = sbuf("scrE3", [128, 2, 512], F32)
        scri = sbuf("scri", [128, 1024], I32)
        cols = sbuf("cols", [128, 64], F32)
        qrow = sbuf("qrow", [16, 8], F32)
        wsc = sbuf("wsc", [128, 8], F32)
        abscol = sbuf("abscol", [128, 4], F32)
        xt = sbuf("xt", [128, 2, D], F32)
        B1 = sbuf("B1", [128, 16, T], BF16)
        B2 = sbuf("B2", [128, 16, T], BF16)

        C = {"sems": {}, "cnt": {}, "waited": {}}

        def sem(name):
            if name not in C["sems"]:
                C["sems"][name] = es.enter_context(nc.semaphore(name))
                C["cnt"][name] = 0
            return C["sems"][name]

        def inc(instr, name, k=1):
            instr.then_inc(sem(name), k)
            C["cnt"][name] += k
            return C["cnt"][name]

        def cnt(name):
            return C["cnt"].get(name, 0)

        def wait(eng, ename, name, val=None):
            if val is None:
                val = cnt(name)
            if val is None or val <= 0:
                return
            key = (ename, name)
            if C["waited"].get(key, -1) >= val:
                return
            eng.wait_ge(sem(name), val)
            C["waited"][key] = val

        if True:
            sp, g, pe, v, a = nc.sync, nc.gpsimd, nc.tensor, nc.vector, nc.scalar

            def vS(instr):
                c = inc(instr, "v")
                wait(v, "v", "v", c)
                return c

            def aS(instr):
                c = inc(instr, "a")
                wait(a, "a", "a", c)
                return c

            def ckpt(n):
                if limit <= n:
                    inc(sp.dma_start(out=out_h[0:128, :], in_=scr1[:, 0:D]),
                        "d_out", 16)
                    wait(sp, "sp", "d_out")
                    raise StopEmission()

            _CKPT = ckpt
            g.memset(ident_bf[:], 0.0)
            g.affine_select(out=ident_bf[:], in_=ident_bf[:],
                            compare_op=ALU.not_equal, fill=1.0, base=0,
                            pattern=[[-1, 128]], channel_multiplier=1)
            g.memset(ident_f32[:], 0.0)
            g.affine_select(out=ident_f32[:], in_=ident_f32[:],
                            compare_op=ALU.not_equal, fill=1.0, base=0,
                            pattern=[[-1, 128]], channel_multiplier=1)
            ckpt(0.005)
            g.memset(ones_bf[:], 1.0)
            g.memset(ones_f32[:], 1.0)
            g.memset(ones_col[:], 1.0)
            ckpt(0.01)
            g.memset(biasA[:], 0.0)
            inc(g.memset(biasB[:], 0.0), "g")
            ckpt(0.02)
            inc(g.dma_start(out=biasA[0:1, :], in_=bq_h), "dc", 16)
            ckpt(0.03)
            inc(g.dma_start(out=biasA[32:33, :], in_=bk_h), "dc", 16)
            inc(g.dma_start(out=biasA[64:65, :], in_=bv_h), "dc", 16)
            inc(g.dma_start(out=biasB[0:1, :], in_=bo_h), "dc", 16)
            ckpt(0.05)

            def ln_stats(src_ap, want_maxmin=False, width=D):
                v.tensor_reduce(cols[:, 0:1], src_ap, AX.X, ALU.add)
                if want_maxmin:
                    v.tensor_reduce(cols[:, 6:7], src_ap, AX.X, ALU.max)
                    v.tensor_reduce(cols[:, 7:8], src_ap, AX.X, ALU.min)
                wait(a, "a", "v")
                inc(a.activation(out=scr1[:, 0:width], in_=src_ap,
                                 func=AF.Square), "a")
                wait(v, "v", "a")
                vS(v.tensor_reduce(cols[:, 1:2], scr1[:, 0:width], AX.X, ALU.add))
                ninv = 1.0 / float(width)
                vS(v.tensor_scalar_mul(cols[:, 2:3], cols[:, 0:1], ninv))
                vS(v.tensor_scalar_mul(cols[:, 3:4], cols[:, 1:2], ninv))
                vS(v.tensor_tensor(cols[:, 8:9], cols[:, 2:3], cols[:, 2:3],
                                   ALU.mult))
                vS(v.tensor_tensor(cols[:, 3:4], cols[:, 3:4], cols[:, 8:9],
                                   ALU.subtract))
                inc(v.tensor_scalar_add(cols[:, 3:4], cols[:, 3:4], EPS_LN), "v")
                wait(a, "a", "v")
                aS(a.activation(out=cols[:, 3:4], in_=cols[:, 3:4], func=AF.Ln))
                inc(a.activation(out=cols[:, 4:5], in_=cols[:, 3:4],
                                 func=AF.Exp, scale=-0.5), "a")
                wait(v, "v", "a")
                vS(v.tensor_scalar_mul(cols[:, 5:6], cols[:, 2:3], -1.0))
                inc(v.tensor_tensor(cols[:, 5:6], cols[:, 5:6], cols[:, 4:5],
                                    ALU.mult), "v")
                wait(v, "v", "v")

            def elu_evac(ps_ap, dst_ap, par=[0]):
                p = par[0] = 1 - par[0]
                e3 = scrE3[:, p, :]
                c_ps = inc(v.tensor_copy(e3, ps_ap), "v")
                v.tensor_scalar_min(scrE1[:], e3, 0.0)
                inc(v.tensor_scalar_max(scrE2[:], e3, 0.0), "v")
                wait(a, "a", "v")
                inc(a.activation(out=scrE1[:], in_=scrE1[:], func=AF.Exp), "a")
                wait(v, "v", "a")
                c_add = inc(v.tensor_add(dst_ap, scrE1[:], scrE2[:]), "v")
                return c_ps, c_add

            def gen_abssum(wq_in):
                sub = 0
                freed = [None, None]
                for mi, (wh, nch, fw) in enumerate([(w1T_h, 16, DFF),
                                                    (w2T_h, 64, D)]):
                    first = True
                    for ch in range(nch):
                        for sc in range(fw // QCH):
                            b = sub % 2
                            if freed[b] is not None:
                                wait(sp, "sp", "g_abs", freed[b])
                            inc(sp.dma_start(
                                out=wq_in[:, b, :],
                                in_=wh[128 * ch:128 * ch + 128,
                                       QCH * sc:QCH * sc + QCH]), "d_wabs", 16)
                            wait(v, "v", "d_wabs")
                            c0 = inc(v.tensor_reduce(
                                abscol[:, b:b + 1], wq_in[:, b, :], AX.X,
                                ALU.add, apply_absolute_value=True), "g_abs")
                            freed[b] = c0
                            wait(v, "v", "g_abs", c0)
                            if first:
                                inc(v.tensor_copy(cols[:, 32 + mi:33 + mi],
                                                  abscol[:, b:b + 1]), "g_abs")
                                first = False
                            else:
                                inc(v.tensor_tensor(
                                    cols[:, 32 + mi:33 + mi],
                                    cols[:, 32 + mi:33 + mi],
                                    abscol[:, b:b + 1], ALU.add), "g_abs")
                            wait(v, "v", "g_abs")
                            sub += 1
                            yield

            def quant_finalize(psSm):
                wait(pe, "pe", "g_abs")
                inc(pe.matmul(psSm[0:1, 0:2], ones_col[:], cols[:, 32:34],
                              start=True, stop=True, skip_group_check=True), "pe")
                wait(v, "v", "pe")
                vS(v.tensor_scalar(qrow[0:1, 0:2], psSm[0:1, 0:2],
                                   1.0 / float(DFF * D), 1e-5, ALU.mult, ALU.max))
                vS(v.reciprocal(qrow[0:1, 2:4], qrow[0:1, 0:2]))
                inc(v.tensor_scalar_mul(qrow[0:1, 4:6], qrow[0:1, 0:2],
                                        1.0 / 127.0), "v")
                wait(pe, "pe", "v")
                pe.matmul(psSm[:, 0:1], ones_f32[0:1, :], qrow[0:1, 2:3],
                          start=True, stop=True, skip_group_check=True)
                pe.matmul(psSm[:, 1:2], ones_f32[0:1, :], qrow[0:1, 4:5],
                          start=False, stop=True, skip_group_check=True)
                pe.matmul(psSm[:, 2:3], ones_f32[0:1, :], qrow[0:1, 3:4],
                          start=False, stop=True, skip_group_check=True)
                inc(pe.matmul(psSm[:, 3:4], ones_f32[0:1, :], qrow[0:1, 5:6],
                              start=False, stop=True, skip_group_check=True), "pe")
                wait(v, "v", "pe")
                inc(v.tensor_copy(wsc[:, 0:4], psSm[:, 0:4]), "v_wsc")
                if "wsc" in taps:
                    wait(sp, "sp", "v_wsc")
                    inc(sp.dma_start(out=tap_t["wsc"], in_=wsc[:]), "d_out", 16)

            def gen_round(wq_in, wq_bfb, qscr):
                sub = 0
                freed = [None, None]
                out_d = [None, None]
                for (wh, scr_t, nch, fw, sci) in [(w1T_h, w1q_scr, 16, DFF, 0),
                                                  (w2T_h, w2q_scr, 64, D, 2)]:
                    for ch in range(nch):
                        for sc in range(fw // QCH):
                            b = sub % 2
                            if freed[b] is not None:
                                wait(sp, "sp", "v_q", freed[b])
                            else:
                                wait(sp, "sp", "g_abs", cnt("g_abs"))
                            inc(sp.dma_start(
                                out=wq_in[:, b, :],
                                in_=wh[128 * ch:128 * ch + 128,
                                       QCH * sc:QCH * sc + QCH]), "d_wrnd", 16)
                            wait(a, "a", "d_wrnd")
                            wait(a, "a", "v_wsc")
                            wait(a, "a", "v_q")
                            inc(a.activation(out=qscr[:], in_=wq_in[:, b, :],
                                             func=AF.Copy,
                                             scale=wsc[:, sci:sci + 1]), "a_q")
                            wait(v, "v", "a_q")
                            v.tensor_scalar_min(qscr[:], qscr[:], 1.0)
                            v.tensor_scalar_max(scri[:, 0:QCH], qscr[:], -1.0)
                            if out_d[b] is not None:
                                wait(v, "v", "d_wqo", out_d[b])
                            freed[b] = inc(v.tensor_copy(wq_bfb[:, b, :],
                                                         scri[:, 0:QCH]), "v_q")
                            wait(sp, "sp", "v_q")
                            if scr_t is w1q_scr:
                                dst = scr_t[sc, ch, :, :]
                            else:
                                dst = scr_t[ch, :, QCH * sc:QCH * sc + QCH]
                            out_d[b] = inc(sp.dma_start(
                                out=dst, in_=wq_bfb[:, b, :]), "d_wqo", 16)
                            sub += 1
                            yield

            def drive(gg, n):
                for _ in range(n):
                    try:
                        next(gg)
                    except StopIteration:
                        return

            def drain(gg):
                for _ in gg:
                    pass

            esK = ExitStack()
            kv_bf = esK.enter_context(nc.sbuf_tensor("kv_bf", [128, H, 132], BF16))
            kblk = esK.enter_context(nc.sbuf_tensor("kblk", [128, H, 16], BF16))
            sel16 = esK.enter_context(nc.sbuf_tensor("sel16", [16, 16, 128], BF16))
            rrow = esK.enter_context(nc.sbuf_tensor("rrow", [16, T], BF16))
            wq_in = esK.enter_context(nc.sbuf_tensor("wq_in", [128, 2, QCH], F32))
            wq_bfb = esK.enter_context(nc.sbuf_tensor("wq_bfb", [128, 2, QCH], BF16))
            qscr = esK.enter_context(nc.sbuf_tensor("qscr", [128, QCH], F32))

            g.memset(kblk[:], 0.0)
            g.memset(sel16[:], 0.0)
            inc(g.affine_select(out=sel16[:], in_=sel16[:],
                                compare_op=ALU.not_equal, fill=1.0, base=0,
                                pattern=[[-1, 16], [0, 128]],
                                channel_multiplier=1), "g")
            ckpt(0.1)

            esA = ExitStack()
            kp_bf = esA.enter_context(nc.sbuf_tensor("kp_bf", [128, 1024], BF16))
            vext = esA.enter_context(nc.sbuf_tensor("vext", [128, 8, 132], BF16))
            xn_tm = esA.enter_context(nc.sbuf_tensor("xn_tm", [128, D], BF16))
            xn_oth = esA.enter_context(nc.sbuf_tensor("xn_oth", [128, 16, 128], BF16))
            psProj = esA.enter_context(nc.psum_tensor("psProj", [128, 2, 512], F32))
            psKV = [esA.enter_context(nc.psum_tensor(f"psKV{i}", [128, 2, 132], F32))
                    for i in range(4)]
            psT_a = esA.enter_context(nc.psum_tensor("psT_a", [128, 4, 128], BF16))
            psT_b = esA.enter_context(nc.psum_tensor("psT_b", [128, 4, 128], BF16))
            psT = [psT_a, psT_b]
            psSm = psProj[:, 0, 0:8]

            inc(g.memset(vext[:], 1.0), "g")

            def load_wkv_pair(p):
                for dc in range(16):
                    inc(g.dma_start(
                        out=B2[:, dc, 0:1024],
                        in_=wkT_h[128 * dc:128 * dc + 128,
                                  1024 * p:1024 * p + 1024]), "dc", 16)
                for dc in range(16):
                    inc(g.dma_start(
                        out=B2[:, dc, 1024:2048],
                        in_=wvT_h[128 * dc:128 * dc + 128,
                                  1024 * p:1024 * p + 1024]), "dc", 16)
                return cnt("dc")

            wkv_ready = load_wkv_pair(0)
            gen_a = gen_abssum(wq_in)
            gen_r = None

            x_freed = [None, None]
            tr_done = [None, None]
            pk_free = None
            pv_free = None
            xn_tm_free = [None]

            def load_x_tile(row, b, src=x_h, tag="d_x"):
                if x_freed[b] is not None:
                    wait(sp, "sp", x_freed[b][0], x_freed[b][1])
                inc(sp.dma_start(out=xt[:, b, :],
                                 in_=src[128 * row:128 * row + 128, :]), tag, 16)

            def build_xn(i, b, dbg=False):
                if dbg:
                    ckpt(0.15)
                wait(v, "v", "d_x")
                wait(a, "a", "d_x")
                ln_stats(xt[:, b, :])
                if dbg:
                    ckpt(0.18)
                if xn_tm_free[0] is not None:
                    wait(v, "v", "pe_tr", xn_tm_free[0])
                c1 = inc(v.tensor_scalar(xn_tm[:], xt[:, b, :], cols[:, 4:5],
                                         cols[:, 5:6], ALU.mult, ALU.add), "v")
                x_freed[b] = ("v", c1)
                if dbg:
                    ckpt(0.19)
                wait(pe, "pe", "v", c1)
                for q in range(4):
                    pj = q % 2
                    if tr_done[pj] is not None:
                        wait(pe, "pe", "v_tr", tr_done[pj])
                    for m in range(4):
                        j = 4 * q + m
                        inc(pe.transpose(psT[pj][:, m, :],
                                         xn_tm[:, 128 * j:128 * j + 128],
                                         ident_bf[:]), "pe_tr")
                    wait(v, "v", "pe_tr")
                    dst = (B1[:, 4 * q:4 * q + 4, 128 * i:128 * i + 128] if i < NT
                           else xn_oth[:, 4 * q:4 * q + 4, :])
                    tr_done[pj] = inc(v.tensor_copy(dst, psT[pj][:]), "v_tr")
                xn_tm_free[0] = cnt("pe_tr")

            for p in range(2):
                if p == 1:
                    if no_quant:
                        ckpt(0.9)
                    quant_finalize(psSm)
                    wait(pe, "pe", "v_wsc")
                    wait(g, "g", "pe")
                    wkv_ready = load_wkv_pair(1)
                    gen_r = gen_round(wq_in, wq_bfb, qscr)
                    wait(pe, "pe", "v_kv")
                for i in range(NTS):
                    b = i % 2
                    own = i < NT
                    if p == 0 or not own:
                        load_x_tile(i, b)
                        build_xn(i, b, dbg=(p == 0 and i == 0))
                    if p == 0 and i == 0:
                        ckpt(0.2)
                    if p == 0 and i == 2:
                        ckpt(0.36)
                    if no_quant:
                        pass
                    elif p == 0:
                        drive(gen_a, 16)
                        if i == NTS - 1:
                            drain(gen_a)
                    else:
                        drive(gen_r, 10)
                    xnsrc = ((lambda dc, i=i: B1[:, dc, 128 * i:128 * i + 128])
                             if own else (lambda dc: xn_oth[:, dc, :]))
                    wait(pe, "pe", "dc", wkv_ready)
                    wait(pe, "pe", "v_tr")
                    for s2 in range(2):
                        strip = 2 * p + s2
                        if pk_free is not None:
                            wait(pe, "pe", "v", pk_free[0])
                            if pk_free[1] is not None:
                                wait(pe, "pe", "a", pk_free[1])
                        for dc in range(16):
                            pe.matmul(psProj[:, 0, :], xnsrc(dc),
                                      B2[:, dc, 512 * s2:512 * s2 + 512],
                                      start=(dc == 0), stop=False,
                                      skip_group_check=True)
                        inc(pe.matmul(psProj[:, 0, :], ones_bf[32:33, 0:128],
                                      biasA[32:33, 512 * strip:512 * strip + 512],
                                      start=False, stop=True,
                                      skip_group_check=True), "pe")
                        wait(v, "v", "pe")
                        cps, kp_add = elu_evac(psProj[:, 0, :],
                                               kp_bf[:, 512 * s2:512 * s2 + 512])
                        pk_free = (cps, None)
                        if pv_free is not None:
                            wait(pe, "pe", "a", pv_free)
                        for dc in range(16):
                            pe.matmul(psProj[:, 1, :], xnsrc(dc),
                                      B2[:, dc, 1024 + 512 * s2:1024 + 512 * s2 + 512],
                                      start=(dc == 0), stop=False,
                                      skip_group_check=True)
                        inc(pe.matmul(psProj[:, 1, :], ones_bf[64:65, 0:128],
                                      biasA[64:65, 512 * strip:512 * strip + 512],
                                      start=False, stop=True,
                                      skip_group_check=True), "pe")
                        wait(a, "a", "pe")
                        pv_free = inc(a.activation(
                            out=vext[:, 4 * s2:4 * s2 + 4, 0:128],
                            in_=psProj[:, 1, :].rearrange("p (h f) -> p h f", h=4),
                            func=AF.Copy), "a")
                        wait(pe, "pe", "v", kp_add)
                        wait(pe, "pe", "a", pv_free)
                        for hh in range(4):
                            pst = psKV[2 * s2 + hh // 2]
                            st = (i == 0 and hh % 2 == 0)
                            mm = pe.matmul(
                                pst[:, hh % 2, 0:129],
                                kp_bf[:, 512 * s2 + 128 * hh:
                                      512 * s2 + 128 * hh + 128],
                                vext[:, 4 * s2 + hh, 0:129],
                                start=st, stop=(i == NTS - 1),
                                skip_group_check=True)
                            if hh == 3:
                                inc(mm, "pe")
                    if p == 0 and i in (0, 1, 2, 3, 15, 17):
                        ckpt({0: 0.3, 1: 0.35, 2: 0.38, 3: 0.4, 15: 0.5, 17: 0.55}[i])
                wait(v, "v", "pe")
                for s2 in range(2):
                    strip = 2 * p + s2
                    for hh in range(4):
                        pst = psKV[2 * s2 + hh // 2]
                        inc(v.tensor_copy(kv_bf[:, 4 * strip + hh, 0:129],
                                          pst[:, hh % 2, 0:129]), "v_kv")
                if p == 0:
                    ckpt(0.7)
            for hh in range(H):
                inc(v.tensor_copy(kblk[:, hh, hh:hh + 1],
                                  kv_bf[:, hh, 128:129]), "v_kv")
            if "xn" in taps:
                wait(sp, "sp", "v_tr")
                inc(sp.dma_start(out=tap_t["xn"],
                                 in_=B1[:].rearrange("p c t -> p (c t)")),
                    "d_out", 16)
            if "kv" in taps:
                wait(sp, "sp", "v_kv")
                inc(sp.dma_start(out=tap_t["kv"],
                                 in_=kv_bf[:].rearrange("p h e -> p (h e)")),
                    "d_out", 16)

            ckpt(1)
            a_end = cnt("pe")
            esA.close()

            esQ = ExitStack()
            wq_sl = esQ.enter_context(nc.sbuf_tensor("wq_sl", [128, 2, 16, 128], BF16))
            psQ = esQ.enter_context(nc.psum_tensor("psQ", [128, 4, 512], F32))

            wait(g, "g", "pe", a_end)

            wq_used = [None, None]

            def load_wq(fc, b):
                if wq_used[b] is not None:
                    wait(g, "g", "pe", wq_used[b])
                inc(g.dma_start(
                    out=wq_sl[:, b, :, :],
                    in_=wqT_h[:, 128 * fc:128 * fc + 128].rearrange(
                        "(c p) f -> p c f", p=128)), "dc", 16)
                return cnt("dc")

            wq_ready = [load_wq(0, 0), None]
            qp_free = [None] * 4
            first_q_evac = True
            wait(pe, "pe", "v_kv")
            wait(pe, "pe", "v")
            wait(pe, "pe", "a")
            for fc in range(16):
                b = fc % 2
                if fc + 1 < 16:
                    wq_ready[(fc + 1) % 2] = load_wq(fc + 1, (fc + 1) % 2)
                wait(pe, "pe", "dc", wq_ready[b])
                for ts4 in range(4):
                    ps = ts4
                    if qp_free[ps] is not None:
                        wait(pe, "pe", "v", qp_free[ps])
                    for dc in range(16):
                        pe.matmul(psQ[:, ps, :], wq_sl[:, b, dc, :],
                                  B1[:, dc, 512 * ts4:512 * ts4 + 512],
                                  start=(dc == 0), stop=False,
                                  skip_group_check=True)
                    inc(pe.matmul(psQ[:, ps, :],
                                  biasA[0:1, 128 * fc:128 * fc + 128],
                                  ones_bf[0:1, 0:512], start=False, stop=True,
                                  skip_group_check=True), "pe")
                    if ts4 == 3:
                        wq_used[b] = cnt("pe")
                    wait(v, "v", "pe")
                    if first_q_evac:
                        wait(v, "v", "pe", a_end)
                        first_q_evac = False
                    qp_free[ps] = elu_evac(psQ[:, ps, :],
                                           B2[:, fc, 512 * ts4:512 * ts4 + 512])[0]
                drive(gen_r, 12)
            drain(gen_r)
            w_all = cnt("d_wqo")
            qe_done = cnt("v")
            q_end = cnt("pe")
            if "qe" in taps:
                wait(sp, "sp", "v", qe_done)
                inc(sp.dma_start(out=tap_t["qe"],
                                 in_=B2[:].rearrange("p c t -> p (c t)")),
                    "d_out", 16)
            ckpt(2)
            esQ.close()

            esZ = ExitStack()
            psZ = esZ.enter_context(nc.psum_tensor("psZ", [16, 512], F32))
            psR = esZ.enter_context(nc.psum_tensor("psR", [128, 2, 512], F32))
            psAt = esZ.enter_context(nc.psum_tensor("psAt", [128, 2, 512], F32))
            rb_sb = esZ.enter_context(nc.sbuf_tensor("rb_sb", [128, 2, 512], F32))

            wait(pe, "pe", "v_kv")
            wait(pe, "pe", "g")
            wait(pe, "pe", "v")
            wait(pe, "pe", "a")
            rr_done = [None] * 4
            for st4 in range(4):
                wait(pe, "pe", "v")
                for c in range(16):
                    mm = pe.matmul(psZ[:], kblk[:, c, :],
                                   B2[:, c, 512 * st4:512 * st4 + 512],
                                   start=(c == 0), stop=(c == 15),
                                   skip_group_check=True)
                inc(mm, "pe")
                wait(v, "v", "pe")
                inc(v.tensor_scalar_add(scr1[0:16, 0:512], psZ[:], EPS_Z), "v")
                wait(a, "a", "v")
                a.activation(out=scr1[0:16, 0:512], in_=scr1[0:16, 0:512],
                             func=AF.Ln)
                rr_done[st4] = inc(a.activation(
                    out=rrow[:, 512 * st4:512 * st4 + 512],
                    in_=scr1[0:16, 0:512], func=AF.Exp, scale=-1.0), "a")
            if "rr" in taps:
                wait(sp, "sp", "a", rr_done[3])
                inc(sp.dma_start(out=tap_t["rr"], in_=rrow[:]), "d_out", 16)

            at_free = [None, None]
            rb_free = [None, None]
            first_at = True
            for st4 in range(4):
                for hh in range(H):
                    sl = hh % 2
                    wait(pe, "pe", "a", rr_done[st4])
                    if rb_free[sl] is not None:
                        wait(pe, "pe", "v", rb_free[sl])
                    pe.matmul(psR[:, sl, :], sel16[:, hh, :],
                              rrow[:, 512 * st4:512 * st4 + 512],
                              start=True, stop=True, skip_group_check=True)
                    if at_free[sl] is not None:
                        wait(pe, "pe", "v", at_free[sl])
                    inc(pe.matmul(psAt[:, sl, :], kv_bf[:, hh, 0:128],
                                  B2[:, hh, 512 * st4:512 * st4 + 512],
                                  start=True, stop=True,
                                  skip_group_check=True), "pe")
                    wait(v, "v", "pe")
                    if first_at:
                        wait(v, "v", "pe", q_end)
                        first_at = False
                    rb_free[sl] = inc(v.tensor_copy(rb_sb[:, sl, :],
                                                    psR[:, sl, :]), "v")
                    at_free[sl] = inc(v.tensor_tensor(
                        B1[:, hh, 512 * st4:512 * st4 + 512],
                        psAt[:, sl, :], rb_sb[:, sl, :], ALU.mult), "v")
            attn_done = cnt("v")
            if "attn" in taps:
                wait(sp, "sp", "v", attn_done)
                inc(sp.dma_start(out=tap_t["attn"],
                                 in_=B1[:].rearrange("p c t -> p (c t)")),
                    "d_out", 16)
            ckpt(3)
            esZ.close()
            esK.close()

            esO = ExitStack()
            psO = esO.enter_context(nc.psum_tensor("psO", [128, 8, 512], F32))
            o_tile = esO.enter_context(nc.sbuf_tensor("o_tile", [128, 2, D], F32))

            wait(g, "g", "pe")
            for hec in range(16):
                inc(g.dma_start(out=B2[:, hec, :],
                                in_=woT_h[128 * hec:128 * hec + 128, :]),
                    "dc", 16)
            wo_ready = cnt("dc")

            o_free = [None, None]
            ot_free = [None, None]
            x1_w = {}
            for tc in range(16):
                ob = tc % 2
                load_x_tile(tc, ob)
                wait(pe, "pe", "dc", wo_ready)
                wait(pe, "pe", "v", attn_done)
                if o_free[ob] is not None:
                    wait(pe, "pe", "a", o_free[ob])
                for gs in range(4):
                    for hec in range(16):
                        pe.matmul(psO[:, 4 * ob + gs, :],
                                  B1[:, hec, 128 * tc:128 * tc + 128],
                                  B2[:, hec, 512 * gs:512 * gs + 512],
                                  start=(hec == 0), stop=False,
                                  skip_group_check=True)
                    inc(pe.matmul(psO[:, 4 * ob + gs, :], ones_bf[0:1, 0:128],
                                  biasB[0:1, 512 * gs:512 * gs + 512],
                                  start=False, stop=True,
                                  skip_group_check=True), "pe")
                wait(a, "a", "pe")
                if ot_free[ob] is not None:
                    wait(a, "a", "d_scr", ot_free[ob])
                o_free[ob] = inc(a.activation(
                    out=o_tile[:, ob, :].rearrange("p (g f) -> p g f", g=4),
                    in_=psO[:, 4 * ob:4 * ob + 4, :], func=AF.Copy), "a")
                wait(v, "v", "a")
                ln_stats(o_tile[:, ob, :])
                v.tensor_scalar(scr1[:, 0:D], o_tile[:, ob, :], cols[:, 4:5],
                                cols[:, 5:6], ALU.mult, ALU.add)
                wait(v, "v", "d_x")
                c1 = inc(v.tensor_add(o_tile[:, ob, :], xt[:, ob, :],
                                      scr1[:, 0:D]), "v")
                x_freed[ob] = ("v", c1)
                wait(sp, "sp", "v", c1)
                x1_w[tc] = inc(sp.dma_start(
                    out=x1_scr[128 * tc:128 * tc + 128, :],
                    in_=o_tile[:, ob, :]), "d_scr", 16)
                ot_free[ob] = x1_w[tc]
            o_end = cnt("pe")
            if "x1" in taps:
                wait(sp, "sp", "d_scr")
                inc(sp.dma_start(out=tap_t["x1"], in_=x1_scr), "d_out", 16)
            ckpt(4)
            esO.close()

            esF = ExitStack()
            w2out = esF.enter_context(nc.sbuf_tensor("w2out", [128, 4, D], F32))
            fcols = esF.enter_context(nc.sbuf_tensor("fcols", [128, 16], F32))

            HAP = lambda j: B2[:, j // 4, 512 * (j % 4):512 * (j % 4) + 512]
            w2out_free = [None] * 4

            for g4 in range(4):
                esF1 = ExitStack()
                psT2_a = esF1.enter_context(nc.psum_tensor(f"psT2a_{g4}", [128, 4, 128], BF16))
                psT2_b = esF1.enter_context(nc.psum_tensor(f"psT2b_{g4}", [128, 4, 128], BF16))
                psT2 = [psT2_a, psT2_b]
                psF = esF1.enter_context(nc.psum_tensor(f"psF_{g4}", [128, 4, 512], F32))
                psX = esF1.enter_context(nc.psum_tensor(f"psX_{g4}", [1, 4, 128], F32))
                for tc in range(4):
                    gt = 4 * g4 + tc
                    b = tc % 2
                    wait(sp, "sp", "d_scr", x1_w[gt])
                    load_x_tile(gt, b, src=x1_scr)
                    wait(v, "v", "d_x")
                    wait(a, "a", "d_x")
                    if g4 == 0 and tc == 0:
                        wait(v, "v", "pe", o_end)
                    if g4 == 0 and tc == 1 and "xtap" in taps:
                        wait(sp, "sp", "d_x")
                        dn = inc(sp.dma_start(out=tap_t["xtap"],
                                              in_=xt[:, b, :]), "d_out", 16)
                        for e_, en_ in ((v, "v"), (a, "a"), (sp, "sp")):
                            wait(e_, en_, "d_out", dn)
                    ln_stats(xt[:, b, :], want_maxmin=True)
                    if g4 == 0 and tc == 1 and "scr1" in taps:
                        wait(sp, "sp", "v")
                        wait(sp, "sp", "a")
                        dn = inc(sp.dma_start(out=tap_t["scr1"], in_=scr1[:]),
                                 "d_out", 16)
                        for e_, en_ in ((v, "v"), (a, "a")):
                            wait(e_, en_, "d_out", dn)
                    if g4 == 0 and tc == 1 and "colsB" in taps:
                        wait(sp, "sp", "v")
                        wait(sp, "sp", "a")
                        dn = inc(sp.dma_start(out=tap_t["colsB"], in_=cols[:]),
                                 "d_out", 16)
                        for e_, en_ in ((v, "v"), (a, "a")):
                            wait(e_, en_, "d_out", dn)
                    vS(v.tensor_tensor(cols[:, 9:10], cols[:, 6:7], cols[:, 2:3],
                                       ALU.subtract))
                    vS(v.tensor_tensor(cols[:, 10:11], cols[:, 2:3], cols[:, 7:8],
                                       ALU.subtract))
                    vS(v.tensor_tensor(cols[:, 9:10], cols[:, 9:10],
                                       cols[:, 10:11], ALU.max))
                    vS(v.tensor_tensor(cols[:, 9:10], cols[:, 9:10],
                                       cols[:, 4:5], ALU.mult))
                    vS(v.tensor_scalar_max(cols[:, 9:10], cols[:, 9:10], 1e-5))
                    if g4 == 0 and tc == 1 and "colsA" in taps:
                        wait(sp, "sp", "v")
                        dn = inc(sp.dma_start(out=tap_t["colsA"], in_=cols[:]),
                                 "d_out", 16)
                        for e_, en_ in ((v, "v"), (a, "a")):
                            wait(e_, en_, "d_out", dn)
                    wait(v, "v", "v_wsc")
                    vS(v.tensor_tensor(cols[:, 16 + tc:17 + tc], cols[:, 9:10],
                                       wsc[:, 1:2], ALU.mult))
                    vS(v.reciprocal(cols[:, 10:11], cols[:, 9:10]))
                    vS(v.tensor_scalar_mul(cols[:, 10:11], cols[:, 10:11], 127.0))
                    vS(v.tensor_tensor(cols[:, 11:12], cols[:, 4:5],
                                       cols[:, 10:11], ALU.mult))
                    inc(v.tensor_tensor(cols[:, 8:9], cols[:, 5:6],
                                        cols[:, 10:11], ALU.mult), "v")
                    wait(v, "v", "v")
                    c1 = inc(v.tensor_scalar(scr1[:, 0:D], xt[:, b, :],
                                             cols[:, 11:12], cols[:, 8:9],
                                             ALU.mult, ALU.add), "v")
                    x_freed[b] = ("v", c1)
                    for hf in range(2):
                        v.tensor_copy(scri[:], scr1[:, 1024 * hf:1024 * hf + 1024])
                        v.tensor_scalar_mul(B1[:, 12 + 2 * hf, 1536:2048],
                                            scri[:, 0:512],
                                            cols[:, 16 + tc:17 + tc])
                        inc(v.tensor_scalar_mul(B1[:, 13 + 2 * hf, 1536:2048],
                                                scri[:, 512:1024],
                                                cols[:, 16 + tc:17 + tc]), "v")
                    wait(pe, "pe", "v", cnt("v"))
                    for q in range(4):
                        pj = q % 2
                        if tr_done[pj] is not None:
                            wait(pe, "pe", "v_tr", tr_done[pj])
                        for m in range(4):
                            inc(pe.transpose(
                                psT2[pj][:, m, :],
                                B1[:, 12 + q, 1536 + 128 * m:1536 + 128 * m + 128],
                                ident_bf[:]), "pe_tr")
                        wait(v, "v", "pe_tr")
                        tr_done[pj] = inc(v.tensor_copy(
                            B1[:, 4 * q:4 * q + 4, 128 * tc:128 * tc + 128],
                            psT2[pj][:]), "v_tr")
                if g4 == 0 and "n1f" in taps:
                    wait(sp, "sp", "v_tr")
                    dn = inc(sp.dma_start(out=tap_t["n1f"],
                             in_=B1[:].rearrange("p c t -> p (c t)")), "d_out", 16)
                    for e_, en_ in ((v, "v"), (a, "a"), (pe, "pe"), (g, "g"),
                                    (sp, "sp")):
                        wait(e_, en_, "d_out", dn)
                if g4 == 0:
                    ckpt(5)
                    wait(sp, "sp", "d_wqo", w_all)

                w1_used = [None, None]

                def load_w1_window(w, b):
                    if w1_used[b] is not None:
                        wait(sp, "sp", "pe", w1_used[b])
                    inc(sp.dma_start(
                        out=B1[:, :, 512 + 512 * b:1024 + 512 * b],
                        in_=w1q_scr[w].rearrange("c p f -> p c f")), "d_w1", 16)
                    return cnt("d_w1")

                w1_ready = [load_w1_window(0, 0), None]
                hs_free = [None] * 4
                first_h = (g4 == 0)
                wait(pe, "pe", "a")
                for w in range(16):
                    b = w % 2
                    if w + 1 < 16:
                        w1_ready[(w + 1) % 2] = load_w1_window(w + 1, (w + 1) % 2)
                    wait(pe, "pe", "d_w1", w1_ready[b])
                    wait(pe, "pe", "v_tr")
                    for fb in range(4):
                        j = 4 * w + fb
                        if hs_free[fb] is not None:
                            wait(pe, "pe", "a", hs_free[fb])
                        for dc in range(16):
                            mm = pe.matmul(
                                psF[:, fb, :],
                                B1[:, dc, 512 + 512 * b + 128 * fb:
                                   512 + 512 * b + 128 * fb + 128],
                                B1[:, dc, 0:512],
                                start=(dc == 0), stop=(dc == 15),
                                skip_group_check=True)
                        inc(mm, "pe")
                        if fb == 3:
                            w1_used[b] = cnt("pe")
                        wait(a, "a", "pe")
                        if first_h:
                            wait(a, "a", "pe", o_end)
                            first_h = False
                        hs_free[fb] = inc(a.activation(
                            out=HAP(j), in_=psF[:, fb, :], func=AF.Silu), "a")
                if g4 == 0 and "hs0" in taps:
                    wait(sp, "sp", "a")
                    dn = inc(sp.dma_start(out=tap_t["hs0"],
                             in_=B2[:].rearrange("p c t -> p (c t)")), "d_out", 16)
                    for e_, en_ in ((v, "v"), (a, "a"), (pe, "pe"), (g, "g"),
                                    (sp, "sp")):
                        wait(e_, en_, "d_out", dn)
                if g4 == 0:
                    ckpt(6)
                f2_end_a = cnt("a")
                mxV, mnV = B1[:, 14, 512:1024], B1[:, 14, 1024:1536]
                wait(v, "v", "a", f2_end_a)
                v.tensor_copy(mxV, HAP(0))
                v.tensor_copy(mnV, HAP(0))
                for j in range(1, 64):
                    v.tensor_tensor(mxV, mxV, HAP(j), ALU.max)
                    v.tensor_tensor(mnV, mnV, HAP(j), ALU.min)
                v.tensor_scalar_mul(mnV, mnV, -1.0)
                inc(v.tensor_tensor(mxV, mxV, mnV, ALU.max), "v")
                for tc in range(4):
                    wait(pe, "pe", "v")
                    inc(pe.transpose(psT2[tc % 2][:, 0, :],
                                     B1[:, 14, 512 + 128 * tc:512 + 128 * tc + 128],
                                     ident_bf[:]), "pe_tr")
                    wait(v, "v", "pe_tr")
                    vS(v.tensor_reduce(fcols[:, tc:tc + 1], psT2[tc % 2][:, 0, :],
                                       AX.X, ALU.max))
                vS(v.tensor_scalar_max(fcols[:, 0:4], fcols[:, 0:4], 1e-5))
                for tc in range(4):
                    vS(v.tensor_tensor(fcols[:, 12 + tc:13 + tc],
                                       fcols[:, tc:tc + 1], wsc[:, 3:4], ALU.mult))
                vS(v.reciprocal(fcols[:, 4:8], fcols[:, 0:4]))
                inc(v.tensor_scalar_mul(fcols[:, 4:8], fcols[:, 4:8], 127.0), "v")
                wait(v, "v", "v")
                wait(pe, "pe", "v")
                for tc in range(4):
                    inc(pe.transpose(psX[0:1, tc, :], fcols[:, 4 + tc:5 + tc],
                                     ident_f32[:]), "pe")
                wait(v, "v", "pe")
                inc(v.tensor_copy(scrE3[0:1, 1, :],
                                  psX[0:1, :, :].rearrange("o t c -> o (t c)")),
                    "v")
                wait(pe, "pe", "v")
                wait(pe, "pe", "a")
                inc(pe.matmul(psF[:, 0, :], ones_f32[0:1, :], scrE3[0:1, 1, :],
                              start=True, stop=True, skip_group_check=True), "pe")
                wait(v, "v", "pe")
                inc(v.tensor_copy(scrE3[:, 0, :], psF[:, 0, :]), "v_s2")
                wait(g, "g", "v_s2")
                wait(v, "v", "v_s2")
                for j in range(64):
                    if j % 2 == 0:
                        v.tensor_tensor(scrE1[:], HAP(j), scrE3[:, 0, :], ALU.mult)
                        v.tensor_copy(scri[:, 0:512], scrE1[:])
                        inc(v.tensor_copy(HAP(j), scri[:, 0:512]), "v_n2")
                    else:
                        g.tensor_tensor(scrE2[:], HAP(j), scrE3[:, 0, :], ALU.mult)
                        g.tensor_copy(scri[:, 512:1024], scrE2[:])
                        inc(g.tensor_copy(HAP(j), scri[:, 512:1024]), "g_n2")
                if g4 == 0 and "h0" in taps:
                    wait(sp, "sp", "v_n2")
                    wait(sp, "sp", "g_n2")
                    inc(sp.dma_start(out=tap_t["h0"],
                                     in_=B2[:].rearrange("p c t -> p (c t)")),
                        "d_out", 16)
                if g4 == 0:
                    ckpt(7)
                esF1.close()
                esF2 = ExitStack()
                psW2 = esF2.enter_context(nc.psum_tensor(f"psW2_{g4}", [128, 8, 512], F32))
                wait(pe, "pe", "v_n2")
                wait(pe, "pe", "g_n2")

                def load_w2(gh, fc, b):
                    inc(sp.dma_start(out=B1[:, 8 + 2 * b, 1536:2048],
                                     in_=w2q_scr[fc, :, 1024 * gh:1024 * gh + 512]),
                        "d_w2", 16)
                    inc(sp.dma_start(out=B1[:, 9 + 2 * b, 1536:2048],
                                     in_=w2q_scr[fc, :,
                                                 1024 * gh + 512:1024 * gh + 1024]),
                        "d_w2", 16)
                    return cnt("d_w2")

                w2_used = [None, None]
                w2_ready = [load_w2(0, 0, 0), None]
                gh_evac = None
                for gh in range(2):
                    for fc in range(64):
                        b = fc % 2
                        nb = (fc + 1) % 2
                        if fc + 1 < 64 or gh == 0:
                            if w2_used[nb] is not None:
                                wait(sp, "sp", "pe", w2_used[nb])
                            w2_ready[nb] = (load_w2(gh, fc + 1, nb)
                                            if fc + 1 < 64 else load_w2(1, 0, nb))
                        wait(pe, "pe", "d_w2", w2_ready[b])
                        if fc == 0 and gh == 1 and gh_evac is not None:
                            wait(pe, "pe", "a", gh_evac)
                        for tc in range(4):
                            for gsub in range(2):
                                mm = pe.matmul(
                                    psW2[:, 2 * tc + gsub, :],
                                    B2[:, fc // 4,
                                       512 * (fc % 4) + 128 * tc:
                                       512 * (fc % 4) + 128 * tc + 128],
                                    B1[:, 8 + 2 * b + gsub, 1536:2048],
                                    start=(fc == 0), stop=(fc == 63),
                                    skip_group_check=True)
                        w2_used[b] = inc(mm, "pe")
                    wait(a, "a", "pe")
                    for tc in range(4):
                        if gh == 0 and w2out_free[tc] is not None:
                            wait(a, "a", "d_out", w2out_free[tc])
                        gh_evac = inc(a.activation(
                            out=w2out[:, tc, 1024 * gh:1024 * gh + 1024]
                            .rearrange("p (s f) -> p s f", s=2),
                            in_=psW2[:, 2 * tc:2 * tc + 2, :],
                            func=AF.Copy, scale=fcols[:, 12 + tc:13 + tc]), "a")
                if g4 == 0:
                    ckpt(8)
                esF2.close()
                for tc in range(4):
                    gt = 4 * g4 + tc
                    b = tc % 2
                    load_x_tile(gt, b, src=x1_scr)
                    wait(v, "v", "a", gh_evac)
                    ln_stats(w2out[:, tc, :])
                    v.tensor_scalar(scr1[:, 0:D], w2out[:, tc, :], cols[:, 4:5],
                                    cols[:, 5:6], ALU.mult, ALU.add)
                    wait(v, "v", "d_x")
                    c1 = inc(v.tensor_add(w2out[:, tc, :], xt[:, b, :],
                                          scr1[:, 0:D]), "v")
                    x_freed[b] = ("v", c1)
                    wait(sp, "sp", "v", c1)
                    w2out_free[tc] = inc(sp.dma_start(
                        out=out_h[128 * gt:128 * gt + 128, :],
                        in_=w2out[:, tc, :]), "d_out", 16)
            wait(sp, "sp", "d_out")
            esF.close()
    except StopEmission:
        pass
    return nc


def kernel(L0_attn_hidden_states, L0_attn_q_w, L0_attn_q_b, L0_attn_k_w,
           L0_attn_k_b, L0_attn_v_w, L0_attn_v_b, L0_attn_o_w, L0_attn_o_b,
           L0_ffn_W1, L0_ffn_W2):
    x = np.asarray(L0_attn_hidden_states, dtype=np.float32)
    ct = lambda w: np.ascontiguousarray(np.asarray(w, dtype=np.float32).T)
    wqT, wkT, wvT, woT = (ct(L0_attn_q_w), ct(L0_attn_k_w),
                          ct(L0_attn_v_w), ct(L0_attn_o_w))
    w1T, w2T = ct(L0_ffn_W1), ct(L0_ffn_W2)
    row = lambda b: np.ascontiguousarray(
        np.asarray(b, dtype=np.float32).reshape(1, D))
    bq, bk, bv, bo = (row(L0_attn_q_b), row(L0_attn_k_b),
                      row(L0_attn_v_b), row(L0_attn_o_b))

    nc = build_program()
    in_maps = []
    for c in range(NCORES):
        b, hp = c // 2, c % 2
        xs = x[b] if hp == 0 else np.ascontiguousarray(
            np.concatenate([x[b, T:], x[b, :T]], axis=0))
        in_maps.append({
            "x": np.ascontiguousarray(xs), "wqT": wqT, "wkT": wkT, "wvT": wvT,
            "woT": woT, "w1T": w1T, "w2T": w2T,
            "bq": bq, "bk": bk, "bv": bv, "bo": bo,
        })
    res = run_bass_kernel_spmd(nc, in_maps, core_ids=list(range(NCORES)))
    out = np.empty((4, S, D), dtype=np.float32)
    for c in range(NCORES):
        b, hp = c // 2, c % 2
        out[b, T * hp:T * hp + T, :] = np.asarray(res.results[c]["out"])
    return out
